# version 09
# baseline (speedup 1.0000x reference)
import contextlib
import numpy as np
import ml_dtypes
import concourse.bass as bass
import concourse.mybir as mybir
from concourse.bass_utils import run_bass_kernel_spmd

F32 = mybir.dt.float32
BF16 = mybir.dt.bfloat16
U32 = mybir.dt.uint32
AF = mybir.ActivationFunctionType
ALU = mybir.AluOpType
AX = mybir.AxisListType

H = 2048
SQ = 4096
NCH = 16
ITER = 5632
NI = 44
SCALE = 0.08838834764831845
EXPOFF = -10.0
EPS = 1e-5
NT = 1024
IDT = 1.0 / 2048.0


class Sem:
    def __init__(self, st, nc, name):
        self.h = st.enter_context(nc.semaphore(name))
        self.n = 0

    def inc(self, instr, k=1):
        instr.then_inc(self.h, k)
        self.n += k
        return self.n


def build_nc(dbg=()):
    nc = bass.Bass("TRN2", target_bir_lowering=False, num_devices=8)

    xT = nc.dram_tensor("xT", [H, SQ], F32, kind="ExternalInput").ap()
    wq = nc.dram_tensor("wq", [H, 512], F32, kind="ExternalInput").ap()
    wk = nc.dram_tensor("wk", [H, 512], F32, kind="ExternalInput").ap()
    wv = nc.dram_tensor("wv", [H, 512], F32, kind="ExternalInput").ap()
    wpT = nc.dram_tensor("wpT", [8 * 128, 4096], F32, kind="ExternalInput").ap()
    bias = nc.dram_tensor("bias", [1, H], F32, kind="ExternalInput").ap()
    gwT = nc.dram_tensor("gwT", [NI * 128, 2048], F32, kind="ExternalInput").ap()
    uwT = nc.dram_tensor("uwT", [NI * 128, 2048], F32, kind="ExternalInput").ap()
    dwT = nc.dram_tensor("dwT", [NI * 128, 2048], F32, kind="ExternalInput").ap()
    gs2d = nc.dram_tensor("gs2d", [128, NI], F32, kind="ExternalInput").ap()
    us2d = nc.dram_tensor("us2d", [128, NI], F32, kind="ExternalInput").ap()
    dsb_in = nc.dram_tensor("dsb_in", [128, H], F32, kind="ExternalInput").ap()
    xrows = nc.dram_tensor("xrows", [NT, H], F32, kind="ExternalInput").ap()
    gidx = nc.dram_tensor("gidx", [64, 1], U32, kind="ExternalInput").ap()
    ones_in = nc.dram_tensor("ones_in", [1, 128], BF16, kind="ExternalInput").ap()
    ident_in = nc.dram_tensor("ident_in", [128, 128], BF16, kind="ExternalInput").ap()
    y = nc.dram_tensor("y", [NT, H], F32, kind="ExternalOutput").ap()

    gw_bf = nc.dram_tensor("gw_bf", [NI * 128, 2048], BF16, kind="Internal").ap()
    uw_bf = nc.dram_tensor("uw_bf", [NI * 128, 2048], BF16, kind="Internal").ap()
    dw_bf = nc.dram_tensor("dw_bf", [4 * 11 * 128, 2048], BF16, kind="Internal").ap()
    wp_bf = nc.dram_tensor("wp_bf", [8 * 128, 4096], BF16, kind="Internal").ap()
    ag_in = nc.dram_tensor("ag_in", [4 * SQ, 128], BF16, kind="Internal").ap()
    ag_out = nc.dram_tensor("ag_out", [16 * SQ, 128], BF16, kind="Internal").ap()
    ctx_loc = nc.dram_tensor("ctx_loc", [16 * 512, 128], BF16, kind="Internal").ap()

    dbg_h = {}
    for name, shape, dt in dbg:
        dbg_h[name] = nc.dram_tensor(name, shape, dt, kind="ExternalOutput").ap()

    st = contextlib.ExitStack()

    consts = st.enter_context(nc.sbuf_tensor("consts", [128, 8], F32))
    stage = st.enter_context(nc.sbuf_tensor("stage", [128, 512], BF16))
    ones_sb = st.enter_context(nc.sbuf_tensor("ones_sb", [1, 128], BF16))
    ident_sb = st.enter_context(nc.sbuf_tensor("ident_sb", [128, 128], BF16))
    bias_sb = st.enter_context(nc.sbuf_tensor("bias_sb", [1, H], BF16))
    gs_sb = st.enter_context(nc.sbuf_tensor("gs_sb", [128, NI], F32))
    us_sb = st.enter_context(nc.sbuf_tensor("us_sb", [128, NI], F32))
    idx_sb = st.enter_context(nc.sbuf_tensor("idx_sb", [64, 1], U32))
    stats = st.enter_context(nc.sbuf_tensor("stats", [128, 8], F32))
    rs_sb = st.enter_context(nc.sbuf_tensor("rs_sb", [128, 4], F32))

    PS4 = st.enter_context(nc.psum_tensor("PS4", [128, 2048], F32))
    PB2 = st.enter_context(nc.psum_tensor("PB2", [128, 1024], F32))
    PT = st.enter_context(nc.psum_tensor("PTp", [128, 2048], BF16))

    sems = {}
    for name in ["wA", "xb0", "xb1", "xf", "peA", "evA", "peB", "exp", "peB2", "rec",
                 "cts", "stg", "cc", "g", "loc", "ct", "wp0", "wp1", "gu0", "gu1",
                 "dw0", "dw1", "xr", "peC", "vC", "sC", "out0", "out1",
                 "pc0", "pc1", "pc2", "pc3", "xr0", "xr1", "pcast", "dbg"]:
        sems[name] = Sem(st, nc, "s_" + name)
    s = sems
    blk = st.enter_context(nc.Block())

    abst = contextlib.ExitStack()
    qT = abst.enter_context(nc.sbuf_tensor("qT", [128, 4 * SQ], BF16))
    kT = abst.enter_context(nc.sbuf_tensor("kT", [128, 4 * SQ], BF16))
    VA = abst.enter_context(nc.sbuf_tensor("VA", [128, 4 * 32 * 129], BF16))

    gath0 = st.enter_context(nc.sbuf_tensor("gath0", [64, 8192], BF16, side="right"))

    ast = contextlib.ExitStack()
    wq_sb = ast.enter_context(nc.sbuf_tensor("wq_sb", [128, NCH * 512], BF16, side="right"))
    wk_sb = ast.enter_context(nc.sbuf_tensor("wk_sb", [128, NCH * 512], BF16, side="right"))
    wv_sb = ast.enter_context(nc.sbuf_tensor("wv_sb", [128, NCH * 512], BF16, side="right"))
    xbuf = [ast.enter_context(nc.sbuf_tensor(f"xbuf{i}", [128, NCH * 512], BF16, side="right"))
            for i in range(2)]

    if True:
        @blk.gpsimd
        def _(g):
            for w_in, w_sb in ((wq, wq_sb), (wk, wk_sb), (wv, wv_sb)):
                src = w_in.rearrange("(ch p) c -> p ch c", p=128)
                dst = w_sb.rearrange("p (ch c) -> p ch c", c=512)
                s["wA"].inc(g.dma_start(out=dst, in_=src), 16)
            s["wA"].inc(g.dma_start(out=bias_sb[:, :], in_=bias[:, :]), 16)

        @blk.sync
        def _(sy):
            s["wA"].inc(sy.dma_start(out=ones_sb[:, :], in_=ones_in[:, :]), 16)
            s["wA"].inc(sy.dma_start(out=ident_sb[:, :], in_=ident_in[:, :]), 16)
            s["wA"].inc(sy.dma_start(out=gs_sb[:, :], in_=gs2d[:, :]), 16)
            s["wA"].inc(sy.dma_start(out=us_sb[:, :], in_=us2d[:, :]), 16)
            s["wA"].inc(sy.dma_start(out=idx_sb[:, :], in_=gidx[:, :]), 16)

        @blk.vector
        def _(v):
            v.memset(consts[:, 0:1], EXPOFF)
            v.memset(consts[:, 1:2], EPS)
            s["evA"].inc(v.memset(VA[:, :], 1.0))

    evA0 = s["evA"].n

    pc_jobs = []
    for w_in, w_out, rows in ((wpT, wp_bf, 8 * 128), (gwT, gw_bf, NI * 128),
                              (uwT, uw_bf, NI * 128), (dwT, dw_bf, NI * 128)):
        step = 512
        for r0 in range(0, rows, step):
            pc_jobs.append((w_in[r0:min(r0 + step, rows), :],
                            w_out[r0:min(r0 + step, rows), :]))
    PCAST_TOTAL = len(pc_jobs) * 16

    def emit_precast(g, lo, hi):
        for j in range(lo, min(hi, len(pc_jobs))):
            src_, dst_ = pc_jobs[j]
            s["pcast"].inc(g.dma_start(out=dst_, in_=src_), 16)

    for sg in range(8):
        if True:
            @blk.gpsimd
            def _(g, sg=sg):
                if sg >= 2:
                    g.wait_ge(s["evA"].h, evA0 + (sg - 2) * 12 + 12)
                src = xT[:, sg * 512:(sg + 1) * 512].rearrange("(ch p) c -> p ch c", p=128)
                dst = xbuf[sg % 2].rearrange("p (ch c) -> p ch c", c=512)
                s["xb%d" % (sg % 2)].inc(g.dma_start(out=dst, in_=src), 16)
                nj = (len(pc_jobs) + 7) // 8
                emit_precast(g, sg * nj, (sg + 1) * nj if sg < 7 else 10 ** 9)

            @blk.tensor
            def _(t, sg=sg):
                t.wait_ge(s["xb%d" % (sg % 2)].h, (sg // 2 + 1) * 16)
                if sg == 0:
                    t.wait_ge(s["wA"].h, s["wA"].n)
                xb = xbuf[sg % 2]
                peA0 = 12 * sg
                for half in range(2):
                    if half == 0:
                        if sg > 0:
                            t.wait_ge(s["evA"].h, evA0 + (sg - 1) * 12 + 8)
                    else:
                        t.wait_ge(s["evA"].h, evA0 + sg * 12 + 4)
                    for qk, w_sb in ((0, wq_sb), (1, wk_sb)):
                        for gl in range(2):
                            gi = half * 2 + gl
                            dst = PS4[:, (qk * 2 + gl) * 512:(qk * 2 + gl + 1) * 512]
                            for ch in range(NCH):
                                ins = t.matmul(dst,
                                               w_sb[:, ch * 512 + gi * 128:ch * 512 + (gi + 1) * 128],
                                               xb[:, ch * 512:(ch + 1) * 512],
                                               start=(ch == 0), stop=(ch == NCH - 1))
                                if ch == NCH - 1:
                                    s["peA"].inc(ins)
                for stt in range(4):
                    if stt < 2:
                        if sg > 0:
                            t.wait_ge(s["evA"].h, evA0 + (sg - 1) * 12 + 11 + stt)
                    else:
                        t.wait_ge(s["evA"].h, evA0 + sg * 12 + 7 + stt)
                    dst = PB2[:, (stt % 2) * 512:(stt % 2 + 1) * 512]
                    for ch in range(NCH):
                        ins = t.matmul(dst,
                                       xb[:, ch * 512 + stt * 128:ch * 512 + (stt + 1) * 128],
                                       wv_sb[:, ch * 512:(ch + 1) * 512],
                                       start=(ch == 0), stop=(ch == NCH - 1))
                        if ch == NCH - 1:
                            s["peA"].inc(ins)

            @blk.vector
            def _(v, sg=sg):
                peA0 = 12 * sg
                order = [("q", 0, 0), ("q", 1, 1), ("k", 0, 2), ("k", 1, 3),
                         ("q", 2, 0), ("q", 3, 1), ("k", 2, 2), ("k", 3, 3)]
                for i, (kind, gi, qtr) in enumerate(order):
                    v.wait_ge(s["peA"].h, peA0 + i + 1)
                    dstT = qT if kind == "q" else kT
                    s["evA"].inc(v.tensor_copy(dstT[:, gi * SQ + sg * 512:gi * SQ + (sg + 1) * 512],
                                               PS4[:, qtr * 512:(qtr + 1) * 512]))
                for stt in range(4):
                    v.wait_ge(s["peA"].h, peA0 + 9 + stt)
                    kc = sg * 4 + stt
                    src = PB2[:, (stt % 2) * 512:(stt % 2 + 1) * 512]
                    ins = None
                    for gi in range(4):
                        ins = v.tensor_copy(VA[:, gi * 32 * 129 + kc * 129:gi * 32 * 129 + kc * 129 + 128],
                                            src[:, gi * 128:(gi + 1) * 128])
                    s["evA"].inc(ins)

    if "d_qT" in dbg_h:
        if True:
            @blk.sync
            def _(sy):
                sy.wait_ge(s["evA"].h, evA0 + 96)
                s["dbg"].inc(sy.dma_start(out=dbg_h["d_qT"][:, :], in_=qT[:, :]), 16)
                s["dbg"].inc(sy.dma_start(out=dbg_h["d_kT"][:, :], in_=kT[:, :]), 16)
                s["dbg"].inc(sy.dma_start(out=dbg_h["d_VA"][:, :], in_=VA[:, :]), 16)

    ast.close()

    bst = contextlib.ExitStack()
    PTb2 = [bst.enter_context(nc.sbuf_tensor("PTb%d" % i, [128, 32 * 512], BF16, side="right"))
            for i in range(2)]

    QORDER = [(gi, qg) for half in (0, 1) for gi in range(4) for qg in range(half, 8, 2)]
    if True:
        @blk.tensor
        def _(t):
            t.wait_ge(s["evA"].h, evA0 + 96)

            def emit_s(QG, kts):
                gi, qg = QORDER[QG]
                exp0 = QG * 16
                for kt in kts:
                    if kt >= 4:
                        w = exp0 + (kt - 4) // 2 + 1
                    else:
                        w = exp0 + (-1, -1, 0, 0)[kt]
                    if w > 0:
                        t.wait_ge(s["exp"].h, w)
                    ins = t.matmul(PS4[:, (kt % 4) * 512:(kt % 4 + 1) * 512],
                                   kT[:, gi * SQ + kt * 128:gi * SQ + (kt + 1) * 128],
                                   qT[:, gi * SQ + qg * 512:gi * SQ + (qg + 1) * 512],
                                   start=True, stop=True)
                    s["peB"].inc(ins)

            emit_s(0, range(32))
            for QG in range(32):
                gi, qg = QORDER[QG]
                exp0 = QG * 16
                cts0 = QG * 4
                for qq in range(4):
                    w = cts0 + qq - 1
                    if w > 0:
                        t.wait_ge(s["cts"].h, w)
                    dst = PB2[:, (qq % 2) * 512:(qq % 2) * 512 + 129]
                    for kc in range(32):
                        if kc % 8 == 0:
                            t.wait_ge(s["exp"].h, exp0 + min(16, kc // 2 + 4))
                        ins = t.matmul(dst,
                                       PTb2[QG % 2][:, kc * 512 + qq * 128:kc * 512 + (qq + 1) * 128],
                                       VA[:, gi * 32 * 129 + kc * 129:gi * 32 * 129 + (kc + 1) * 129],
                                       start=(kc == 0), stop=(kc == 31))
                        if kc == 31:
                            s["peB2"].inc(ins)
                    if QG + 1 < 32:
                        emit_s(QG + 1, range(qq * 8, qq * 8 + 8))

        @blk.scalar
        def _(sc):
            sc.wait_ge(s["evA"].h, evA0)
            for QG in range(32):
                peB0 = QG * 32
                guard = (QG - 1) * 4
                if guard > 0:
                    sc.wait_ge(s["peB2"].h, guard)
                for j in range(16):
                    sc.wait_ge(s["peB"].h, peB0 + 2 * j + 2)
                    bk = (2 * j) % 4
                    ins = sc.activation(out=PTb2[QG % 2][:, 2 * j * 512:(2 * j + 2) * 512],
                                        in_=PS4[:, bk * 512:(bk + 2) * 512],
                                        func=AF.Exp, bias=consts[:, 0:1], scale=SCALE)
                    s["exp"].inc(ins)

        @blk.vector
        def _(v):
            for QG in range(32):
                peB20 = QG * 4
                stg0 = QG * 16
                if QG > 0:
                    v.wait_ge(s["stg"].h, stg0)
                for qq in range(4):
                    v.wait_ge(s["peB2"].h, peB20 + qq + 1)
                    src_ = PB2[:, (qq % 2) * 512:(qq % 2) * 512 + 129]
                    s["rec"].inc(v.reciprocal(rs_sb[:, qq:qq + 1], src_[:, 128:129]))
                    v.wait_ge(s["rec"].h, s["rec"].n)
                    ins = v.tensor_scalar(stage[:, qq * 128:(qq + 1) * 128],
                                          src_[:, 0:128], rs_sb[:, qq:qq + 1], None, ALU.mult)
                    s["cts"].inc(ins)

        @blk.sync
        def _(sy):
            for QG in range(32):
                gi, qg = QORDER[QG]
                half, q4 = qg % 2, qg // 2
                base = half * 8192 + gi * 2048 + q4 * 512
                sy.wait_ge(s["cts"].h, QG * 4 + 4)
                dst = ag_in[base:base + 512, :].rearrange("(q p) d -> p q d", p=128)
                src_ = stage.rearrange("p (q d) -> p q d", d=128)
                s["stg"].inc(sy.dma_start(out=dst, in_=src_), 16)

    if "d_agin" in dbg_h:
        if True:
            @blk.sync
            def _(sy):
                sy.wait_ge(s["stg"].h, s["stg"].n)
                s["dbg"].inc(sy.dma_start(out=dbg_h["d_agin"][:, :], in_=ag_in[:, :]), 16)

    bst.close()

    gw_sb = [st.enter_context(nc.sbuf_tensor(f"gw{i}", [128, NCH * 128], BF16, side="right")) for i in range(2)]
    uw_sb = [st.enter_context(nc.sbuf_tensor(f"uw{i}", [128, NCH * 128], BF16, side="right")) for i in range(2)]
    dw_sb = [st.enter_context(nc.sbuf_tensor(f"dw{i}", [128, 4 * 512], BF16, side="right")) for i in range(2)]
    wp_sb = [st.enter_context(nc.sbuf_tensor(f"wp{i}", [128, NCH * 256], BF16, side="right")) for i in range(2)]
    dsb_sb = st.enter_context(nc.sbuf_tensor("dsb_sb", [128, H], F32, side="right"))

    wp_fill = {}
    gu_fill = {}
    dw_fill = {}
    wp_use = {}
    gu_use = {}
    dw_use = {}

    pcast_waited = [False]

    def _pcast_gate(g):
        if not pcast_waited[0]:
            g.wait_ge(s["pcast"].h, PCAST_TOTAL)
            pcast_waited[0] = True

    def load_wp(g, tg, ho):
        slot = (tg * 8 + ho) % 2
        if slot in wp_use:
            g.wait_ge(s["peC"].h, wp_use[slot])
        sem = s["wp%d" % slot]
        sem.inc(g.dma_start(out=wp_sb[slot][:, :], in_=wp_bf[ho * 128:(ho + 1) * 128, :]), 16)
        wp_fill[(tg, ho)] = (sem, sem.n)

    def load_gu(g, tg, it):
        slot = it % 2
        if slot in gu_use:
            g.wait_ge(s["peC"].h, gu_use[slot])
        sem = s["gu%d" % slot]
        sem.inc(g.dma_start(out=gw_sb[slot][:, :], in_=gw_bf[it * 128:(it + 1) * 128, :]), 16)
        sem.inc(g.dma_start(out=uw_sb[slot][:, :], in_=uw_bf[it * 128:(it + 1) * 128, :]), 16)
        gu_fill[(tg, it)] = (sem, sem.n)

    def load_dw(g, tg, hh, jt):
        slot = (hh * 11 + jt) % 2
        if slot in dw_use:
            g.wait_ge(s["peC"].h, dw_use[slot])
        sem = s["dw%d" % slot]
        sem.inc(g.dma_start(out=dw_sb[slot][:, :],
                            in_=dw_bf[(hh * 11 + jt) * 128:(hh * 11 + jt + 1) * 128, :]), 16)
        dw_fill[(tg, hh, jt)] = (sem, sem.n)

    if True:
        @blk.gpsimd
        def _(g):
            g.wait_ge(s["stg"].h, 256)
            ins = g.collective_compute(
                "AllGather", ALU.bypass,
                replica_groups=[[0, 1, 2, 3], [4, 5, 6, 7]],
                ins=[ag_in[0:8192, :].rearrange("a b -> (a b)")[None, :]],
                outs=[ag_out[0:32768, :].rearrange("a b -> (a b)")[None, :]],
            )
            s["cc"].inc(ins)
            g.wait_ge(s["cc"].h, 1)
            for half in range(2):
                if half == 1:
                    g.wait_ge(s["loc"].h, 256)
                ins = g.indirect_dma_start(
                    out=gath0[:, :], out_offset=None,
                    in_=ag_out.rearrange("(c r) d -> c (r d)", r=64),
                    in_offset=bass.IndirectOffsetOnAxis(idx_sb[:, 0:1], 0),
                    element_offset=half * 8192,
                )
                s["g"].inc(ins, 16)
                g.wait_ge(s["g"].h, 16 * (half + 1))
                for hb in range(16):
                    src_ = gath0[hb * 4:(hb + 1) * 4, :].rearrange("q (j d) -> q j d", d=128)
                    dst_ = ctx_loc[hb * 512:(hb + 1) * 512, :] \
                        .rearrange("(q j) d -> q j d", j=128)[:, half * 64:(half + 1) * 64, :]
                    s["loc"].inc(g.dma_start(out=dst_, in_=src_), 16)
            g.wait_ge(s["stg"].h, 512)
            ins = g.collective_compute(
                "AllGather", ALU.bypass,
                replica_groups=[[0, 1, 2, 3], [4, 5, 6, 7]],
                ins=[ag_in[8192:16384, :].rearrange("a b -> (a b)")[None, :]],
                outs=[ag_out[32768:65536, :].rearrange("a b -> (a b)")[None, :]],
            )
            s["cc"].inc(ins)
            _pcast_gate(g)
            load_wp(g, 0, 0)
            load_wp(g, 0, 1)
            load_gu(g, 0, 0)
            load_gu(g, 0, 1)

        @blk.sync
        def _(sy):
            s["wA"].inc(sy.dma_start(out=dsb_sb[:, :], in_=dsb_in[:, :]), 16)

    abst.close()
    cst = contextlib.ExitStack()
    z_tg = cst.enter_context(nc.sbuf_tensor("z_tg", [128, 4 * H], F32))
    xr2 = [cst.enter_context(nc.sbuf_tensor("xr2_%d" % i, [128, H], F32)) for i in range(2)]
    a1T = cst.enter_context(nc.sbuf_tensor("a1T", [128, NCH * 512], BF16))
    bnst = cst.enter_context(nc.sbuf_tensor("bnst", [128, 24], F32))
    out_sb = [cst.enter_context(nc.sbuf_tensor("out0", [128, H], F32))] * 2
    sg_bf = cst.enter_context(nc.sbuf_tensor("sg_bf", [128, 512], BF16))

    proj_evac = {}
    out_dma = {}
    pt_use = {}

    proj_done = {}
    xr_fill = {}
    xr_use = {}
    for tg in range(2):
        g0 = s["g"].n
        loc0 = s["loc"].n
        peC_j = s["peC"].n
        vC_j = s["vC"].n
        sC_j = s["sC"].n
        hstk = contextlib.ExitStack()
        hdnT = hstk.enter_context(nc.sbuf_tensor("hdnT%d" % tg, [128, NI * 512], BF16))
        if tg == 1:
            if True:
                @blk.gpsimd
                def _(g, tg=tg, loc0=loc0):
                    g.wait_ge(s["cc"].h, 2)
                    g.wait_ge(s["loc"].h, loc0)
                    for half in range(2):
                        if half == 1:
                            g.wait_ge(s["loc"].h, loc0 + 256)
                        ins = g.indirect_dma_start(
                            out=gath0[:, :], out_offset=None,
                            in_=ag_out.rearrange("(c r) d -> c (r d)", r=64),
                            in_offset=bass.IndirectOffsetOnAxis(idx_sb[:, 0:1], 0),
                            element_offset=tg * 512 * 8192 + half * 8192,
                        )
                        s["g"].inc(ins, 16)
                        g.wait_ge(s["g"].h, s["g"].n)
                        for hb in range(16):
                            src_ = gath0[hb * 4:(hb + 1) * 4, :].rearrange("q (j d) -> q j d", d=128)
                            dst_ = ctx_loc[hb * 512:(hb + 1) * 512, :] \
                                .rearrange("(q j) d -> q j d", j=128)[:, half * 64:(half + 1) * 64, :]
                            s["loc"].inc(g.dma_start(out=dst_, in_=src_), 16)

        cstk = contextlib.ExitStack()
        ctxT = cstk.enter_context(nc.sbuf_tensor("ctxT%d" % tg, [128, NCH * 512], BF16))
        a1bf = cstk.enter_context(nc.sbuf_tensor("a1bf%d" % tg, [128, H], BF16))
        if True:
            if True:
                @blk.sync
                def _(sy, tg=tg):
                    sy.wait_ge(s["loc"].h, s["loc"].n)
                    if tg == 1:
                        sy.wait_ge(s["peC"].h, proj_done[0])
                    for hb in range(16):
                        s["ct"].inc(sy.dma_start_transpose(out=ctxT[:, hb * 512:(hb + 1) * 512],
                                                           in_=ctx_loc[hb * 512:(hb + 1) * 512, :]), 16)

                @blk.scalar
                def _(sc, tg=tg):
                    for tt in range(2):
                        sl = tt % 2
                        if (tg, tt) in xr_use:
                            sc.wait_ge(s["vC"].h, xr_use[(tg, tt)])
                        sem = s["xr%d" % sl]
                        sem.inc(sc.dma_start(
                            out=xr2[sl][:, :],
                            in_=xrows[tg * 512 + tt * 128:tg * 512 + (tt + 1) * 128, :]), 16)
                        xr_fill[(tg, tt)] = (sem, sem.n)

            ct_done = s["ct"].n

            if tg == 0 and "d_ctxT" in dbg_h:
                if True:
                    @blk.scalar
                    def _(sc, ct_done=ct_done):
                        sc.wait_ge(s["ct"].h, ct_done)
                        s["dbg"].inc(sc.dma_start(out=dbg_h["d_ctxT"][:, :], in_=ctxT[:, :]), 16)

            for ho in range(8):
                peC0 = s["peC"].n
                if True:
                    @blk.tensor
                    def _(t, tg=tg, ho=ho, ct_done=ct_done):
                        fs, fv = wp_fill[(tg, ho)]
                        t.wait_ge(fs.h, fv)
                        if ho == 0:
                            t.wait_ge(s["ct"].h, ct_done)
                        for tt in range(4):
                            key = (tg, ho - 1, tt) if ho > 0 else (tg - 1, 7, tt)
                            if key in proj_evac:
                                t.wait_ge(s["vC"].h, proj_evac[key])
                            dst = PS4[:, tt * 512:tt * 512 + 256]
                            for ch in range(NCH):
                                t.matmul(dst,
                                         ctxT[:, ch * 512 + tt * 128:ch * 512 + (tt + 1) * 128],
                                         wp_sb[(tg * 8 + ho) % 2][:, ch * 256:(ch + 1) * 256],
                                         start=(ch == 0), stop=False)
                            ins = t.matmul(dst, ones_sb[0:1, :],
                                           bias_sb[0:1, ho * 256:(ho + 1) * 256],
                                           start=False, stop=True)
                            s["peC"].inc(ins)

                    @blk.gpsimd
                    def _(g, tg=tg, ho=ho):
                        nxt = tg * 8 + ho + 2
                        if nxt < 16:
                            wp_use[(tg * 8 + ho) % 2] = s["peC"].n
                            load_wp(g, nxt // 8, nxt % 8)

                    @blk.vector
                    def _(v, tg=tg, ho=ho, peC0=peC0):
                        for tt in range(4):
                            v.wait_ge(s["peC"].h, peC0 + tt + 1)
                            src = PS4[:, tt * 512:tt * 512 + 256]
                            ins = v.tensor_copy(
                                z_tg[:, tt * H + ho * 256:tt * H + (ho + 1) * 256], src)
                            proj_evac[(tg, ho, tt)] = s["vC"].inc(ins)

            proj_done[tg] = s["peC"].n
            sC_l1 = s["sC"].n
            vC0s_l1 = []
            if True:
                @blk.vector
                def _(v, tg=tg):
                    for tt in range(2):
                        zt = z_tg[:, tt * H:(tt + 1) * H]
                        fs, fv = xr_fill[(tg, tt)]
                        v.wait_ge(fs.h, fv)
                        for hoq in range(8):
                            q0 = proj_evac.get((tg, hoq, tt))
                        v.wait_ge(s["vC"].h, proj_evac[(tg, 7, tt)])
                        s["vC"].inc(v.tensor_tensor(zt, zt, xr2[tt % 2][:, :], ALU.add))
                        xr_use[(tg, tt + 2)] = s["vC"].n

                @blk.scalar
                def _(sc, tg=tg):
                    for tt in (2, 3):
                        sl = tt % 2
                        sc.wait_ge(s["vC"].h, xr_use[(tg, tt)])
                        sem = s["xr%d" % sl]
                        sem.inc(sc.dma_start(
                            out=xr2[sl][:, :],
                            in_=xrows[tg * 512 + tt * 128:tg * 512 + (tt + 1) * 128, :]), 16)
                        xr_fill[(tg, tt)] = (sem, sem.n)

                @blk.vector
                def _(v, tg=tg):
                    for tt in (2, 3):
                        zt = z_tg[:, tt * H:(tt + 1) * H]
                        fs, fv = xr_fill[(tg, tt)]
                        v.wait_ge(fs.h, fv)
                        v.wait_ge(s["vC"].h, proj_evac[(tg, 7, tt)])
                        s["vC"].inc(v.tensor_tensor(zt, zt, xr2[tt % 2][:, :], ALU.add))
                        xr_use[(tg + 1, tt - 2)] = s["vC"].n

            for tt in range(4):
                vC0 = s["vC"].n
                vC0s_l1.append(vC0)
                sC0 = sC_l1 + 2 * tt
                if True:
                    @blk.vector
                    def _(v, tt=tt, vC0=vC0, sC0=sC0):
                        zt = z_tg[:, tt * H:(tt + 1) * H]
                        _ln_vector(nc, s, v, zt, zt, vC0, sC0, stats, bnst)
                        v.wait_ge(s["vC"].h, s["vC"].n)
                        s["vC"].inc(v.tensor_copy(a1bf[:, :], zt))

                vC0b = s["vC"].n
                peC0 = s["peC"].n
                if True:
                    @blk.tensor
                    def _(t, tt=tt, vC0b=vC0b):
                        t.wait_ge(s["vC"].h, vC0b)
                        for half in range(2):
                            if half in pt_use:
                                t.wait_ge(s["vC"].h, pt_use[half])
                            for j in range(8):
                                ch = half * 8 + j
                                ins = t.transpose(PT[:, half * 1024 + j * 128:half * 1024 + (j + 1) * 128],
                                                  a1bf[:, ch * 128:(ch + 1) * 128], ident_sb[:, :])
                                if j == 7:
                                    s["peC"].inc(ins)

                    @blk.vector
                    def _(v, tt=tt, peC0=peC0):
                        for half in range(2):
                            v.wait_ge(s["peC"].h, peC0 + half + 1)
                            src = PT[:, half * 1024:(half + 1) * 1024].rearrange("p (j d) -> p j d", d=128)
                            dst = a1T.rearrange("p (ch c) -> p ch c", c=512)[:, half * 8:(half + 1) * 8,
                                                                            tt * 128:(tt + 1) * 128]
                            pt_use[half] = s["vC"].inc(v.tensor_copy(dst, src))

        if True:
            @blk.scalar
            def _(sc, vC0s=tuple(vC0s_l1), sC_l1=sC_l1):
                for tt in range(4):
                    zt = z_tg[:, tt * H:(tt + 1) * H]
                    _ln_scalar(nc, s, sc, zt, vC0s[tt], sC_l1 + 2 * tt, stats, consts)

        if tg == 0 and "d_a1" in dbg_h:
            if True:
                @blk.scalar
                def _(sc, v1=s["vC"].n):
                    sc.wait_ge(s["vC"].h, v1)
                    s["dbg"].inc(sc.dma_start(out=dbg_h["d_a1"][:, :], in_=z_tg[:, :]), 16)
                    s["dbg"].inc(sc.dma_start(out=dbg_h["d_a1T"][:, :], in_=a1T[:, :]), 16)

        cstk.close()

        if True:
            a1T_ready = s["vC"].n
            peC_f = s["peC"].n
            sC_f = s["sC"].n
            vC_f = s["vC"].n
            if True:
                @blk.gpsimd
                def _(g, tg=tg, peC_f=peC_f):
                    for it in range(2, NI):
                        gu_use[it % 2] = peC_f + (it - 1) * 2
                        load_gu(g, tg, it)
                    if tg == 0:
                        for it2 in range(2):
                            gu_use[it2 % 2] = peC_f + (NI - 1 + it2) * 2
                            load_gu(g, 1, it2)

                @blk.tensor
                def _(t, tg=tg, a1T_ready=a1T_ready, vC_f=vC_f, sC_f=sC_f):
                    for it in range(NI):
                        fs, fv = gu_fill[(tg, it)]
                        t.wait_ge(fs.h, fv)
                        if it == 0:
                            t.wait_ge(s["vC"].h, a1T_ready)
                        if it >= 2:
                            t.wait_ge(s["vC"].h, vC_f + it - 1)
                            t.wait_ge(s["sC"].h, sC_f + it - 1)
                        base = (it % 2) * 1024
                        for which, w_sb, off in (("g", gw_sb, 0), ("u", uw_sb, 512)):
                            for ch in range(NCH):
                                ins = t.matmul(PS4[:, base + off:base + off + 512],
                                               w_sb[it % 2][:, ch * 128:(ch + 1) * 128],
                                               a1T[:, ch * 512:(ch + 1) * 512],
                                               start=(ch == 0), stop=(ch == NCH - 1))
                                if ch == NCH - 1:
                                    s["peC"].inc(ins)

                @blk.scalar
                def _(sc, peC_f=peC_f, vC_f=vC_f):
                    for it in range(NI):
                        sc.wait_ge(s["vC"].h, vC_f + it)
                        sc.wait_ge(s["peC"].h, peC_f + 2 * it + 1)
                        ins = sc.activation(out=sg_bf[:, :],
                                            in_=PS4[:, (it % 2) * 1024:(it % 2) * 1024 + 512],
                                            func=AF.Silu, scale=gs_sb[:, it:it + 1])
                        s["sC"].inc(ins)

                @blk.vector
                def _(v, peC_f=peC_f, sC_f=sC_f):
                    for it in range(NI):
                        v.wait_ge(s["peC"].h, peC_f + 2 * it + 2)
                        v.wait_ge(s["sC"].h, sC_f + it + 1)
                        ins = v.scalar_tensor_tensor(
                            out=hdnT[:, it * 512:(it + 1) * 512],
                            in0=PS4[:, (it % 2) * 1024 + 512:(it % 2) * 1024 + 1024],
                            scalar=us_sb[:, it:it + 1], in1=sg_bf[:, :],
                            op0=ALU.mult, op1=ALU.mult)
                        s["vC"].inc(ins)

            hdn_all = s["vC"].n
            if tg == 0 and "d_hdnT" in dbg_h:
                if True:
                    @blk.scalar
                    def _(sc, hdn_all=hdn_all):
                        sc.wait_ge(s["vC"].h, hdn_all)
                        s["dbg"].inc(sc.dma_start(out=dbg_h["d_hdnT"][:, :], in_=hdnT[:, :]), 16)
            peC_d = s["peC"].n
            vC_d = s["vC"].n

            def _dcnt(nn):
                hh, jt = nn // 11, nn % 11
                return peC_d + hh * 14 + (jt + 1 if jt < 10 else 14)

            if True:
                @blk.gpsimd
                def _(g, tg=tg):
                    for nn in range(44):
                        if nn >= 2:
                            dw_use[nn % 2] = _dcnt(nn - 2)
                        load_dw(g, tg, (nn // 11), (nn % 11))
                    dw_use[0] = _dcnt(42)
                    dw_use[1] = _dcnt(43)

                @blk.tensor
                def _(t, tg=tg, hdn_all=hdn_all, vC_d=vC_d):
                    for hh in range(4):
                        for jt in range(11):
                            fs, fv = dw_fill[(tg, hh, jt)]
                            t.wait_ge(fs.h, fv)
                            if hh == 0 and jt == 0:
                                t.wait_ge(s["vC"].h, hdn_all)
                            if jt == 0 and hh > 0:
                                t.wait_ge(s["vC"].h, vC_d + (hh - 1) * 8 + 8)
                            for itl in range(4):
                                it = jt * 4 + itl
                                for tt in range(4):
                                    dst = PS4[:, tt * 512:(tt + 1) * 512]
                                    ins = t.matmul(dst,
                                                   hdnT[:, it * 512 + tt * 128:it * 512 + (tt + 1) * 128],
                                                   dw_sb[(hh * 11 + jt) % 2][:, itl * 512:(itl + 1) * 512],
                                                   start=(it == 0), stop=(it == NI - 1))
                                    if it == NI - 1:
                                        s["peC"].inc(ins)
                                    elif itl == 3 and tt == 3:
                                        s["peC"].inc(ins)

                @blk.vector
                def _(v, tg=tg):
                    if tg == 1:
                        v.wait_ge(s["out0"].h, 64)
                    for hh in range(4):
                        v.wait_ge(s["peC"].h, _dcnt(hh * 11 + 10))
                        for tt in range(4):
                            src_ = PS4[:, tt * 512:(tt + 1) * 512]
                            t2 = out_sb[0][:, hh * 512:(hh + 1) * 512]
                            v.wait_ge(s["vC"].h, s["vC"].n)
                            s["vC"].inc(v.tensor_tensor(t2, src_,
                                                        dsb_sb[:, hh * 512:(hh + 1) * 512], ALU.mult))
                            v.wait_ge(s["vC"].h, s["vC"].n)
                            ins = v.scalar_tensor_tensor(
                                out=z_tg[:, tt * H + hh * 512:tt * H + (hh + 1) * 512],
                                in0=t2, scalar=1.0,
                                in1=z_tg[:, tt * H + hh * 512:tt * H + (hh + 1) * 512],
                                op0=ALU.mult, op1=ALU.add)
                            s["vC"].inc(ins)

            if tg == 0 and "d_z2" in dbg_h:
                if True:
                    @blk.scalar
                    def _(sc, v1=s["vC"].n):
                        sc.wait_ge(s["vC"].h, v1)
                        s["dbg"].inc(sc.dma_start(out=dbg_h["d_z2"][:, :], in_=z_tg[:, :]), 16)

            sC_l2 = s["sC"].n
            vC0s_l2 = []
            apply_vals = []
            for tt in range(4):
                vC0 = s["vC"].n
                vC0s_l2.append(vC0)
                sC0 = sC_l2 + 2 * tt
                if True:
                    @blk.vector
                    def _(v, tg=tg, tt=tt, vC0=vC0, sC0=sC0):
                        zt = z_tg[:, tt * H:(tt + 1) * H]
                        prev_uses = 4 * tg + tt
                        if prev_uses > 0:
                            v.wait_ge(s["out0"].h, 16 * prev_uses)
                        _ln_vector(nc, s, v, zt, out_sb[tt % 2][:, :], vC0, sC0, stats, bnst)
                apply_vals.append(s["vC"].n)

            if True:
                @blk.scalar
                def _(sc, tg=tg, vC0s=tuple(vC0s_l2), sC_l2=sC_l2, applies=tuple(apply_vals)):
                    for tt in range(4):
                        zt = z_tg[:, tt * H:(tt + 1) * H]
                        _ln_scalar(nc, s, sc, zt, vC0s[tt], sC_l2 + 2 * tt, stats, consts)
                        sc.wait_ge(s["vC"].h, applies[tt])
                        sem = s["out0"]
                        sem.inc(
                            sc.dma_start(out=y[tg * 512 + tt * 128:tg * 512 + (tt + 1) * 128, :],
                                         in_=out_sb[tt % 2][:, :]), 16)
                        out_dma[(tg, tt)] = (sem, sem.n)

        hstk.close()

    if True:
        @blk.sync
        def _(sy):
            sy.wait_ge(s["out0"].h, s["out0"].n)
            sy.wait_ge(s["out1"].h, s["out1"].n)
            if dbg_h:
                sy.wait_ge(s["dbg"].h, s["dbg"].n)

        @blk.gpsimd
        def _(g):
            g.wait_ge(s["out0"].h, s["out0"].n)
            g.wait_ge(s["out1"].h, s["out1"].n)

    cst.close()
    st.close()
    return nc, dbg_h


def _ln_vector(nc, s, v, zt, out_ap, vC0, sC0, stats, bnst):
    v.wait_ge(s["vC"].h, vC0)
    ins = None
    for k in range(4):
        ins = v.bn_stats(bnst[:, k * 6:(k + 1) * 6], zt[:, k * 512:(k + 1) * 512])
    s["vC"].inc(ins)
    v.wait_ge(s["vC"].h, s["vC"].n)
    s["vC"].inc(v.bn_aggr(stats[:, 0:2], bnst[:, 0:24]))
    v.wait_ge(s["sC"].h, sC0 + 2)
    s["vC"].inc(v.tensor_scalar(out_ap, zt, stats[:, 0:1], stats[:, 3:4],
                                ALU.subtract, ALU.mult))


def _ln_scalar(nc, s, sc, zt, vC0, sC0, stats, consts):
    sc.wait_ge(s["vC"].h, vC0 + 2)
    s["sC"].inc(sc.activation(out=stats[:, 2:3], in_=stats[:, 1:2],
                              func=AF.Ln, bias=consts[:, 1:2]))
    sc.wait_ge(s["sC"].h, s["sC"].n)
    s["sC"].inc(sc.activation(out=stats[:, 3:4], in_=stats[:, 2:3],
                              func=AF.Exp, scale=-0.5))


def _host_inputs(inputs):
    x = np.ascontiguousarray(np.asarray(inputs["L0_attn_query"], dtype=np.float32))
    Wqkv = np.asarray(inputs["L0_attn_Wqkv"], dtype=np.float32)
    Wproj = np.asarray(inputs["L0_attn_Wproj"], dtype=np.float32)
    bproj = np.asarray(inputs["L0_attn_bproj"], dtype=np.float32)
    gate_w = np.asarray(inputs["L0_ffn_gate_w"], dtype=np.float32)
    up_w = np.asarray(inputs["L0_ffn_up_w"], dtype=np.float32)
    down_w = np.asarray(inputs["L0_ffn_down_w"], dtype=np.float32)
    gate_s = np.asarray(inputs["L0_ffn_gate_s"], dtype=np.float32)
    up_s = np.asarray(inputs["L0_ffn_up_s"], dtype=np.float32)
    down_s = np.asarray(inputs["L0_ffn_down_s"], dtype=np.float32)

    WqkvT = np.ascontiguousarray(Wqkv.T)
    xTb = [np.ascontiguousarray(x[b].T) for b in range(2)]
    wpT = np.ascontiguousarray(Wproj.reshape(8, 256, 16, 128).transpose(0, 3, 2, 1).reshape(8 * 128, 4096))
    gwT = np.ascontiguousarray(gate_w.reshape(44, 128, 16, 128).transpose(0, 3, 2, 1).reshape(44 * 128, 2048))
    uwT = np.ascontiguousarray(up_w.reshape(44, 128, 16, 128).transpose(0, 3, 2, 1).reshape(44 * 128, 2048))
    dwT = np.ascontiguousarray(down_w.reshape(4, 512, 11, 4, 128).transpose(0, 2, 4, 3, 1).reshape(44 * 128, 2048))
    gs2d = np.ascontiguousarray(gate_s.reshape(NI, 128).T)
    us2d = np.ascontiguousarray(up_s.reshape(NI, 128).T)
    dsb = np.ascontiguousarray(np.broadcast_to(down_s, (128, H)))
    bias2d = np.ascontiguousarray(bproj.reshape(1, H))
    ones_bf = np.ones((1, 128), dtype=ml_dtypes.bfloat16)
    ident_bf = np.eye(128, dtype=ml_dtypes.bfloat16)

    in_maps = []
    for c in range(8):
        beta = (c % 4) // 2
        par = 1 if c >= 4 else 0
        heads = [8 * (c % 2) + 2 * gi + par for gi in range(4)]
        q = c % 4

        def slab(base):
            cols = [WqkvT[:, h * 384 + base:h * 384 + base + 128] for h in heads]
            return np.ascontiguousarray(np.concatenate(cols, axis=1))

        idx = np.zeros((64, 1), dtype=np.uint32)
        for p in range(64):
            rank, gi, sub = p // 16, (p // 4) % 4, p % 4
            idx[p, 0] = rank * 128 + gi * 32 + q * 8 + sub * 2
        bp = c // 4
        in_maps.append({
            "xT": xTb[beta],
            "wq": slab(0), "wk": slab(128), "wv": slab(256),
            "wpT": wpT, "bias": bias2d,
            "gwT": gwT, "uwT": uwT, "dwT": dwT,
            "gs2d": gs2d, "us2d": us2d, "dsb_in": dsb,
            "xrows": np.ascontiguousarray(x[bp, q * 1024:(q + 1) * 1024, :]),
            "gidx": idx,
            "ones_in": ones_bf, "ident_in": ident_bf,
        })
    return in_maps


def kernel(**inputs):
    in_maps = _host_inputs(inputs)
    nc, _ = build_nc()
    res = run_bass_kernel_spmd(nc, in_maps, core_ids=list(range(8)))
    out = np.empty((2, 4096, 2048), dtype=np.float32)
    for c in range(8):
        bp, q = c // 4, c % 4
        out[bp, q * 1024:(q + 1) * 1024, :] = np.asarray(res.results[c]["y"])
    return out


if __name__ == "__main__":
    nc, _ = build_nc()
    print("built ok, instructions:", nc.n_instructions() if callable(getattr(nc, "n_instructions", None)) else "?")


# version 10
# speedup vs baseline: 1.0051x; 1.0051x over previous
import contextlib
import numpy as np
import ml_dtypes
import concourse.bass as bass
import concourse.mybir as mybir
from concourse.bass_utils import run_bass_kernel_spmd

F32 = mybir.dt.float32
BF16 = mybir.dt.bfloat16
U32 = mybir.dt.uint32
AF = mybir.ActivationFunctionType
ALU = mybir.AluOpType
AX = mybir.AxisListType

H = 2048
SQ = 4096
NCH = 16
ITER = 5632
NI = 44
SCALE = 0.08838834764831845
EXPOFF = -10.0
EPS = 1e-5
NT = 1024
IDT = 1.0 / 2048.0


class Sem:
    def __init__(self, st, nc, name):
        self.h = st.enter_context(nc.semaphore(name))
        self.n = 0

    def inc(self, instr, k=1):
        instr.then_inc(self.h, k)
        self.n += k
        return self.n


def build_nc(dbg=()):
    nc = bass.Bass("TRN2", target_bir_lowering=False, num_devices=8)

    xT = nc.dram_tensor("xT", [H, SQ], F32, kind="ExternalInput").ap()
    wq = nc.dram_tensor("wq", [H, 512], F32, kind="ExternalInput").ap()
    wk = nc.dram_tensor("wk", [H, 512], F32, kind="ExternalInput").ap()
    wv = nc.dram_tensor("wv", [H, 512], F32, kind="ExternalInput").ap()
    wpT = nc.dram_tensor("wpT", [8 * 128, 4096], F32, kind="ExternalInput").ap()
    bias = nc.dram_tensor("bias", [1, H], F32, kind="ExternalInput").ap()
    gwT = nc.dram_tensor("gwT", [NI * 128, 2048], F32, kind="ExternalInput").ap()
    uwT = nc.dram_tensor("uwT", [NI * 128, 2048], F32, kind="ExternalInput").ap()
    dwT = nc.dram_tensor("dwT", [NI * 128, 2048], F32, kind="ExternalInput").ap()
    gs2d = nc.dram_tensor("gs2d", [128, NI], F32, kind="ExternalInput").ap()
    us2d = nc.dram_tensor("us2d", [128, NI], F32, kind="ExternalInput").ap()
    dsb_in = nc.dram_tensor("dsb_in", [128, H], F32, kind="ExternalInput").ap()
    xrows = nc.dram_tensor("xrows", [NT, H], F32, kind="ExternalInput").ap()
    gidx = nc.dram_tensor("gidx", [64, 1], U32, kind="ExternalInput").ap()
    ones_in = nc.dram_tensor("ones_in", [1, 128], BF16, kind="ExternalInput").ap()
    ident_in = nc.dram_tensor("ident_in", [128, 128], BF16, kind="ExternalInput").ap()
    y = nc.dram_tensor("y", [NT, H], F32, kind="ExternalOutput").ap()

    gw_bf = nc.dram_tensor("gw_bf", [NI * 128, 2048], BF16, kind="Internal").ap()
    uw_bf = nc.dram_tensor("uw_bf", [NI * 128, 2048], BF16, kind="Internal").ap()
    dw_bf = nc.dram_tensor("dw_bf", [4 * 11 * 128, 2048], BF16, kind="Internal").ap()
    wp_bf = nc.dram_tensor("wp_bf", [8 * 128, 4096], BF16, kind="Internal").ap()
    ag_in = nc.dram_tensor("ag_in", [4 * SQ, 128], BF16, kind="Internal").ap()
    ag_out = nc.dram_tensor("ag_out", [16 * SQ, 128], BF16, kind="Internal").ap()
    ctx_loc = nc.dram_tensor("ctx_loc", [16 * 512, 128], BF16, kind="Internal").ap()

    dbg_h = {}
    for name, shape, dt in dbg:
        dbg_h[name] = nc.dram_tensor(name, shape, dt, kind="ExternalOutput").ap()

    st = contextlib.ExitStack()

    consts = st.enter_context(nc.sbuf_tensor("consts", [128, 8], F32))
    stage = st.enter_context(nc.sbuf_tensor("stage", [128, 512], BF16))
    ones_sb = st.enter_context(nc.sbuf_tensor("ones_sb", [1, 128], BF16))
    ident_sb = st.enter_context(nc.sbuf_tensor("ident_sb", [128, 128], BF16))
    bias_sb = st.enter_context(nc.sbuf_tensor("bias_sb", [1, H], BF16))
    gs_sb = st.enter_context(nc.sbuf_tensor("gs_sb", [128, NI], F32))
    us_sb = st.enter_context(nc.sbuf_tensor("us_sb", [128, NI], F32))
    idx_sb = st.enter_context(nc.sbuf_tensor("idx_sb", [64, 1], U32))
    stats = st.enter_context(nc.sbuf_tensor("stats", [128, 8], F32))
    rs_sb = st.enter_context(nc.sbuf_tensor("rs_sb", [128, 4], F32))

    PS4 = st.enter_context(nc.psum_tensor("PS4", [128, 2048], F32))
    PB2 = st.enter_context(nc.psum_tensor("PB2", [128, 1024], F32))
    PT = st.enter_context(nc.psum_tensor("PTp", [128, 2048], BF16))

    sems = {}
    for name in ["wA", "xb0", "xb1", "xf", "peA", "evA", "peB", "exp", "peB2", "rec",
                 "cts", "stg", "cc", "g", "loc", "ct", "wp0", "wp1", "gu0", "gu1",
                 "dw0", "dw1", "xr", "peC", "vC", "sC", "out0", "out1",
                 "pc0", "pc1", "pc2", "pc3", "xr0", "xr1", "pcast", "dbg"]:
        sems[name] = Sem(st, nc, "s_" + name)
    s = sems
    blk = st.enter_context(nc.Block())

    abst = contextlib.ExitStack()
    qT = abst.enter_context(nc.sbuf_tensor("qT", [128, 4 * SQ], BF16))
    kT = abst.enter_context(nc.sbuf_tensor("kT", [128, 4 * SQ], BF16))
    VA = abst.enter_context(nc.sbuf_tensor("VA", [128, 4 * 32 * 129], BF16))

    gath0 = st.enter_context(nc.sbuf_tensor("gath0", [64, 8192], BF16, side="right"))

    ast = contextlib.ExitStack()
    wq_sb = ast.enter_context(nc.sbuf_tensor("wq_sb", [128, NCH * 512], BF16, side="right"))
    wk_sb = ast.enter_context(nc.sbuf_tensor("wk_sb", [128, NCH * 512], BF16, side="right"))
    wv_sb = ast.enter_context(nc.sbuf_tensor("wv_sb", [128, NCH * 512], BF16, side="right"))
    xbuf = [ast.enter_context(nc.sbuf_tensor(f"xbuf{i}", [128, NCH * 512], BF16, side="right"))
            for i in range(2)]

    if True:
        @blk.gpsimd
        def _(g):
            for w_in, w_sb in ((wq, wq_sb), (wk, wk_sb), (wv, wv_sb)):
                src = w_in.rearrange("(ch p) c -> p ch c", p=128)
                dst = w_sb.rearrange("p (ch c) -> p ch c", c=512)
                s["wA"].inc(g.dma_start(out=dst, in_=src), 16)
            s["wA"].inc(g.dma_start(out=bias_sb[:, :], in_=bias[:, :]), 16)

        @blk.sync
        def _(sy):
            s["wA"].inc(sy.dma_start(out=ones_sb[:, :], in_=ones_in[:, :]), 16)
            s["wA"].inc(sy.dma_start(out=ident_sb[:, :], in_=ident_in[:, :]), 16)
            s["wA"].inc(sy.dma_start(out=gs_sb[:, :], in_=gs2d[:, :]), 16)
            s["wA"].inc(sy.dma_start(out=us_sb[:, :], in_=us2d[:, :]), 16)
            s["wA"].inc(sy.dma_start(out=idx_sb[:, :], in_=gidx[:, :]), 16)

        @blk.vector
        def _(v):
            v.memset(consts[:, 0:1], EXPOFF)
            v.memset(consts[:, 1:2], EPS)
            s["evA"].inc(v.memset(VA[:, :], 1.0))

    evA0 = s["evA"].n

    pc_jobs = []
    for w_in, w_out, rows in ((wpT, wp_bf, 8 * 128), (gwT, gw_bf, NI * 128),
                              (uwT, uw_bf, NI * 128), (dwT, dw_bf, NI * 128)):
        step = 512
        for r0 in range(0, rows, step):
            pc_jobs.append((w_in[r0:min(r0 + step, rows), :],
                            w_out[r0:min(r0 + step, rows), :]))
    PCAST_TOTAL = len(pc_jobs) * 16

    def emit_precast(g, lo, hi):
        for j in range(lo, min(hi, len(pc_jobs))):
            src_, dst_ = pc_jobs[j]
            s["pcast"].inc(g.dma_start(out=dst_, in_=src_), 16)

    for sg in range(8):
        if True:
            @blk.gpsimd
            def _(g, sg=sg):
                if sg >= 2:
                    g.wait_ge(s["evA"].h, evA0 + (sg - 2) * 12 + 12)
                src = xT[:, sg * 512:(sg + 1) * 512].rearrange("(ch p) c -> p ch c", p=128)
                dst = xbuf[sg % 2].rearrange("p (ch c) -> p ch c", c=512)
                s["xb%d" % (sg % 2)].inc(g.dma_start(out=dst, in_=src), 16)
                emit_precast(g, sg * 2, sg * 2 + 2)

            @blk.tensor
            def _(t, sg=sg):
                t.wait_ge(s["xb%d" % (sg % 2)].h, (sg // 2 + 1) * 16)
                if sg == 0:
                    t.wait_ge(s["wA"].h, s["wA"].n)
                xb = xbuf[sg % 2]
                peA0 = 12 * sg
                for half in range(2):
                    if half == 0:
                        if sg > 0:
                            t.wait_ge(s["evA"].h, evA0 + (sg - 1) * 12 + 8)
                    else:
                        t.wait_ge(s["evA"].h, evA0 + sg * 12 + 4)
                    for qk, w_sb in ((0, wq_sb), (1, wk_sb)):
                        for gl in range(2):
                            gi = half * 2 + gl
                            dst = PS4[:, (qk * 2 + gl) * 512:(qk * 2 + gl + 1) * 512]
                            for ch in range(NCH):
                                ins = t.matmul(dst,
                                               w_sb[:, ch * 512 + gi * 128:ch * 512 + (gi + 1) * 128],
                                               xb[:, ch * 512:(ch + 1) * 512],
                                               start=(ch == 0), stop=(ch == NCH - 1))
                                if ch == NCH - 1:
                                    s["peA"].inc(ins)
                for stt in range(4):
                    if stt < 2:
                        if sg > 0:
                            t.wait_ge(s["evA"].h, evA0 + (sg - 1) * 12 + 11 + stt)
                    else:
                        t.wait_ge(s["evA"].h, evA0 + sg * 12 + 7 + stt)
                    dst = PB2[:, (stt % 2) * 512:(stt % 2 + 1) * 512]
                    for ch in range(NCH):
                        ins = t.matmul(dst,
                                       xb[:, ch * 512 + stt * 128:ch * 512 + (stt + 1) * 128],
                                       wv_sb[:, ch * 512:(ch + 1) * 512],
                                       start=(ch == 0), stop=(ch == NCH - 1))
                        if ch == NCH - 1:
                            s["peA"].inc(ins)

            @blk.vector
            def _(v, sg=sg):
                peA0 = 12 * sg
                order = [("q", 0, 0), ("q", 1, 1), ("k", 0, 2), ("k", 1, 3),
                         ("q", 2, 0), ("q", 3, 1), ("k", 2, 2), ("k", 3, 3)]
                for i, (kind, gi, qtr) in enumerate(order):
                    v.wait_ge(s["peA"].h, peA0 + i + 1)
                    dstT = qT if kind == "q" else kT
                    s["evA"].inc(v.tensor_copy(dstT[:, gi * SQ + sg * 512:gi * SQ + (sg + 1) * 512],
                                               PS4[:, qtr * 512:(qtr + 1) * 512]))
                for stt in range(4):
                    v.wait_ge(s["peA"].h, peA0 + 9 + stt)
                    kc = sg * 4 + stt
                    src = PB2[:, (stt % 2) * 512:(stt % 2 + 1) * 512]
                    ins = None
                    for gi in range(4):
                        ins = v.tensor_copy(VA[:, gi * 32 * 129 + kc * 129:gi * 32 * 129 + kc * 129 + 128],
                                            src[:, gi * 128:(gi + 1) * 128])
                    s["evA"].inc(ins)

    if True:
        @blk.gpsimd
        def _(g):
            emit_precast(g, 16, 10 ** 9)

    if "d_qT" in dbg_h:
        if True:
            @blk.sync
            def _(sy):
                sy.wait_ge(s["evA"].h, evA0 + 96)
                s["dbg"].inc(sy.dma_start(out=dbg_h["d_qT"][:, :], in_=qT[:, :]), 16)
                s["dbg"].inc(sy.dma_start(out=dbg_h["d_kT"][:, :], in_=kT[:, :]), 16)
                s["dbg"].inc(sy.dma_start(out=dbg_h["d_VA"][:, :], in_=VA[:, :]), 16)

    ast.close()

    bst = contextlib.ExitStack()
    PTb2 = [bst.enter_context(nc.sbuf_tensor("PTb%d" % i, [128, 32 * 512], BF16, side="right"))
            for i in range(2)]

    QORDER = [(gi, qg) for half in (0, 1) for gi in range(4) for qg in range(half, 8, 2)]
    if True:
        @blk.tensor
        def _(t):
            t.wait_ge(s["evA"].h, evA0 + 96)

            def emit_s(QG, kts):
                gi, qg = QORDER[QG]
                exp0 = QG * 16
                for kt in kts:
                    if kt >= 4:
                        w = exp0 + (kt - 4) // 2 + 1
                    else:
                        w = exp0 + (-1, -1, 0, 0)[kt]
                    if w > 0:
                        t.wait_ge(s["exp"].h, w)
                    ins = t.matmul(PS4[:, (kt % 4) * 512:(kt % 4 + 1) * 512],
                                   kT[:, gi * SQ + kt * 128:gi * SQ + (kt + 1) * 128],
                                   qT[:, gi * SQ + qg * 512:gi * SQ + (qg + 1) * 512],
                                   start=True, stop=True)
                    s["peB"].inc(ins)

            emit_s(0, range(32))
            for QG in range(32):
                gi, qg = QORDER[QG]
                exp0 = QG * 16
                cts0 = QG * 4
                for qq in range(4):
                    w = cts0 + qq - 1
                    if w > 0:
                        t.wait_ge(s["cts"].h, w)
                    dst = PB2[:, (qq % 2) * 512:(qq % 2) * 512 + 129]
                    for kc in range(32):
                        if kc % 8 == 0:
                            t.wait_ge(s["exp"].h, exp0 + min(16, kc // 2 + 4))
                        ins = t.matmul(dst,
                                       PTb2[QG % 2][:, kc * 512 + qq * 128:kc * 512 + (qq + 1) * 128],
                                       VA[:, gi * 32 * 129 + kc * 129:gi * 32 * 129 + (kc + 1) * 129],
                                       start=(kc == 0), stop=(kc == 31))
                        if kc == 31:
                            s["peB2"].inc(ins)
                    if QG + 1 < 32:
                        emit_s(QG + 1, range(qq * 8, qq * 8 + 8))

        @blk.scalar
        def _(sc):
            sc.wait_ge(s["evA"].h, evA0)
            for QG in range(32):
                peB0 = QG * 32
                guard = (QG - 1) * 4
                if guard > 0:
                    sc.wait_ge(s["peB2"].h, guard)
                for j in range(16):
                    sc.wait_ge(s["peB"].h, peB0 + 2 * j + 2)
                    bk = (2 * j) % 4
                    ins = sc.activation(out=PTb2[QG % 2][:, 2 * j * 512:(2 * j + 2) * 512],
                                        in_=PS4[:, bk * 512:(bk + 2) * 512],
                                        func=AF.Exp, bias=consts[:, 0:1], scale=SCALE)
                    s["exp"].inc(ins)

        @blk.vector
        def _(v):
            for QG in range(32):
                peB20 = QG * 4
                stg0 = QG * 16
                if QG > 0:
                    v.wait_ge(s["stg"].h, stg0)
                for qq in range(4):
                    v.wait_ge(s["peB2"].h, peB20 + qq + 1)
                    src_ = PB2[:, (qq % 2) * 512:(qq % 2) * 512 + 129]
                    s["rec"].inc(v.reciprocal(rs_sb[:, qq:qq + 1], src_[:, 128:129]))
                    v.wait_ge(s["rec"].h, s["rec"].n)
                    ins = v.tensor_scalar(stage[:, qq * 128:(qq + 1) * 128],
                                          src_[:, 0:128], rs_sb[:, qq:qq + 1], None, ALU.mult)
                    s["cts"].inc(ins)

        @blk.sync
        def _(sy):
            for QG in range(32):
                gi, qg = QORDER[QG]
                half, q4 = qg % 2, qg // 2
                base = half * 8192 + gi * 2048 + q4 * 512
                sy.wait_ge(s["cts"].h, QG * 4 + 4)
                dst = ag_in[base:base + 512, :].rearrange("(q p) d -> p q d", p=128)
                src_ = stage.rearrange("p (q d) -> p q d", d=128)
                s["stg"].inc(sy.dma_start(out=dst, in_=src_), 16)

    if "d_agin" in dbg_h:
        if True:
            @blk.sync
            def _(sy):
                sy.wait_ge(s["stg"].h, s["stg"].n)
                s["dbg"].inc(sy.dma_start(out=dbg_h["d_agin"][:, :], in_=ag_in[:, :]), 16)

    bst.close()

    gw_sb = [st.enter_context(nc.sbuf_tensor(f"gw{i}", [128, NCH * 128], BF16, side="right")) for i in range(2)]
    uw_sb = [st.enter_context(nc.sbuf_tensor(f"uw{i}", [128, NCH * 128], BF16, side="right")) for i in range(2)]
    dw_sb = [st.enter_context(nc.sbuf_tensor(f"dw{i}", [128, 4 * 512], BF16, side="right")) for i in range(2)]
    wp_sb = [st.enter_context(nc.sbuf_tensor(f"wp{i}", [128, NCH * 256], BF16, side="right")) for i in range(2)]
    dsb_sb = st.enter_context(nc.sbuf_tensor("dsb_sb", [128, H], F32, side="right"))

    wp_fill = {}
    gu_fill = {}
    dw_fill = {}
    wp_use = {}
    gu_use = {}
    dw_use = {}

    pcast_waited = [False]

    def _pcast_gate(g):
        if not pcast_waited[0]:
            g.wait_ge(s["pcast"].h, PCAST_TOTAL)
            pcast_waited[0] = True

    def load_wp(g, tg, ho):
        slot = (tg * 8 + ho) % 2
        if slot in wp_use:
            g.wait_ge(s["peC"].h, wp_use[slot])
        sem = s["wp%d" % slot]
        sem.inc(g.dma_start(out=wp_sb[slot][:, :], in_=wp_bf[ho * 128:(ho + 1) * 128, :]), 16)
        wp_fill[(tg, ho)] = (sem, sem.n)

    def load_gu(g, tg, it):
        slot = it % 2
        if slot in gu_use:
            g.wait_ge(s["peC"].h, gu_use[slot])
        sem = s["gu%d" % slot]
        sem.inc(g.dma_start(out=gw_sb[slot][:, :], in_=gw_bf[it * 128:(it + 1) * 128, :]), 16)
        sem.inc(g.dma_start(out=uw_sb[slot][:, :], in_=uw_bf[it * 128:(it + 1) * 128, :]), 16)
        gu_fill[(tg, it)] = (sem, sem.n)

    def load_dw(g, tg, hh, jt):
        slot = (hh * 11 + jt) % 2
        if slot in dw_use:
            g.wait_ge(s["peC"].h, dw_use[slot])
        sem = s["dw%d" % slot]
        sem.inc(g.dma_start(out=dw_sb[slot][:, :],
                            in_=dw_bf[(hh * 11 + jt) * 128:(hh * 11 + jt + 1) * 128, :]), 16)
        dw_fill[(tg, hh, jt)] = (sem, sem.n)

    if True:
        @blk.gpsimd
        def _(g):
            g.wait_ge(s["stg"].h, 256)
            ins = g.collective_compute(
                "AllGather", ALU.bypass,
                replica_groups=[[0, 1, 2, 3], [4, 5, 6, 7]],
                ins=[ag_in[0:8192, :].rearrange("a b -> (a b)")[None, :]],
                outs=[ag_out[0:32768, :].rearrange("a b -> (a b)")[None, :]],
            )
            s["cc"].inc(ins)
            g.wait_ge(s["cc"].h, 1)
            for half in range(2):
                if half == 1:
                    g.wait_ge(s["loc"].h, 256)
                ins = g.indirect_dma_start(
                    out=gath0[:, :], out_offset=None,
                    in_=ag_out.rearrange("(c r) d -> c (r d)", r=64),
                    in_offset=bass.IndirectOffsetOnAxis(idx_sb[:, 0:1], 0),
                    element_offset=half * 8192,
                )
                s["g"].inc(ins, 16)
                g.wait_ge(s["g"].h, 16 * (half + 1))
                for hb in range(16):
                    src_ = gath0[hb * 4:(hb + 1) * 4, :].rearrange("q (j d) -> q j d", d=128)
                    dst_ = ctx_loc[hb * 512:(hb + 1) * 512, :] \
                        .rearrange("(q j) d -> q j d", j=128)[:, half * 64:(half + 1) * 64, :]
                    s["loc"].inc(g.dma_start(out=dst_, in_=src_), 16)
            g.wait_ge(s["stg"].h, 512)
            ins = g.collective_compute(
                "AllGather", ALU.bypass,
                replica_groups=[[0, 1, 2, 3], [4, 5, 6, 7]],
                ins=[ag_in[8192:16384, :].rearrange("a b -> (a b)")[None, :]],
                outs=[ag_out[32768:65536, :].rearrange("a b -> (a b)")[None, :]],
            )
            s["cc"].inc(ins)
            _pcast_gate(g)
            load_wp(g, 0, 0)
            load_wp(g, 0, 1)
            load_gu(g, 0, 0)
            load_gu(g, 0, 1)

        @blk.sync
        def _(sy):
            s["wA"].inc(sy.dma_start(out=dsb_sb[:, :], in_=dsb_in[:, :]), 16)

    abst.close()
    cst = contextlib.ExitStack()
    z_tg = cst.enter_context(nc.sbuf_tensor("z_tg", [128, 4 * H], F32))
    xr2 = [cst.enter_context(nc.sbuf_tensor("xr2_%d" % i, [128, H], F32)) for i in range(2)]
    a1T = cst.enter_context(nc.sbuf_tensor("a1T", [128, NCH * 512], BF16))
    bnst = cst.enter_context(nc.sbuf_tensor("bnst", [128, 24], F32))
    out_sb = [cst.enter_context(nc.sbuf_tensor("out0", [128, H], F32))] * 2
    sg_bf = cst.enter_context(nc.sbuf_tensor("sg_bf", [128, 512], BF16))

    proj_evac = {}
    out_dma = {}
    pt_use = {}

    proj_done = {}
    xr_fill = {}
    xr_use = {}
    for tg in range(2):
        g0 = s["g"].n
        loc0 = s["loc"].n
        peC_j = s["peC"].n
        vC_j = s["vC"].n
        sC_j = s["sC"].n
        hstk = contextlib.ExitStack()
        hdnT = hstk.enter_context(nc.sbuf_tensor("hdnT%d" % tg, [128, NI * 512], BF16))
        if tg == 1:
            if True:
                @blk.gpsimd
                def _(g, tg=tg, loc0=loc0):
                    g.wait_ge(s["cc"].h, 2)
                    g.wait_ge(s["loc"].h, loc0)
                    for half in range(2):
                        if half == 1:
                            g.wait_ge(s["loc"].h, loc0 + 256)
                        ins = g.indirect_dma_start(
                            out=gath0[:, :], out_offset=None,
                            in_=ag_out.rearrange("(c r) d -> c (r d)", r=64),
                            in_offset=bass.IndirectOffsetOnAxis(idx_sb[:, 0:1], 0),
                            element_offset=tg * 512 * 8192 + half * 8192,
                        )
                        s["g"].inc(ins, 16)
                        g.wait_ge(s["g"].h, s["g"].n)
                        for hb in range(16):
                            src_ = gath0[hb * 4:(hb + 1) * 4, :].rearrange("q (j d) -> q j d", d=128)
                            dst_ = ctx_loc[hb * 512:(hb + 1) * 512, :] \
                                .rearrange("(q j) d -> q j d", j=128)[:, half * 64:(half + 1) * 64, :]
                            s["loc"].inc(g.dma_start(out=dst_, in_=src_), 16)

        cstk = contextlib.ExitStack()
        ctxT = cstk.enter_context(nc.sbuf_tensor("ctxT%d" % tg, [128, NCH * 512], BF16))
        a1bf = cstk.enter_context(nc.sbuf_tensor("a1bf%d" % tg, [128, H], BF16))
        if True:
            if True:
                @blk.sync
                def _(sy, tg=tg):
                    sy.wait_ge(s["loc"].h, s["loc"].n)
                    if tg == 1:
                        sy.wait_ge(s["peC"].h, proj_done[0])
                    for hb in range(16):
                        s["ct"].inc(sy.dma_start_transpose(out=ctxT[:, hb * 512:(hb + 1) * 512],
                                                           in_=ctx_loc[hb * 512:(hb + 1) * 512, :]), 16)

                @blk.scalar
                def _(sc, tg=tg):
                    for tt in range(2):
                        sl = tt % 2
                        if (tg, tt) in xr_use:
                            sc.wait_ge(s["vC"].h, xr_use[(tg, tt)])
                        sem = s["xr%d" % sl]
                        sem.inc(sc.dma_start(
                            out=xr2[sl][:, :],
                            in_=xrows[tg * 512 + tt * 128:tg * 512 + (tt + 1) * 128, :]), 16)
                        xr_fill[(tg, tt)] = (sem, sem.n)

            ct_done = s["ct"].n

            if tg == 0 and "d_ctxT" in dbg_h:
                if True:
                    @blk.scalar
                    def _(sc, ct_done=ct_done):
                        sc.wait_ge(s["ct"].h, ct_done)
                        s["dbg"].inc(sc.dma_start(out=dbg_h["d_ctxT"][:, :], in_=ctxT[:, :]), 16)

            for ho in range(8):
                peC0 = s["peC"].n
                if True:
                    @blk.tensor
                    def _(t, tg=tg, ho=ho, ct_done=ct_done):
                        fs, fv = wp_fill[(tg, ho)]
                        t.wait_ge(fs.h, fv)
                        if ho == 0:
                            t.wait_ge(s["ct"].h, ct_done)
                        for tt in range(4):
                            key = (tg, ho - 1, tt) if ho > 0 else (tg - 1, 7, tt)
                            if key in proj_evac:
                                t.wait_ge(s["vC"].h, proj_evac[key])
                            dst = PS4[:, tt * 512:tt * 512 + 256]
                            for ch in range(NCH):
                                t.matmul(dst,
                                         ctxT[:, ch * 512 + tt * 128:ch * 512 + (tt + 1) * 128],
                                         wp_sb[(tg * 8 + ho) % 2][:, ch * 256:(ch + 1) * 256],
                                         start=(ch == 0), stop=False)
                            ins = t.matmul(dst, ones_sb[0:1, :],
                                           bias_sb[0:1, ho * 256:(ho + 1) * 256],
                                           start=False, stop=True)
                            s["peC"].inc(ins)

                    @blk.gpsimd
                    def _(g, tg=tg, ho=ho):
                        nxt = tg * 8 + ho + 2
                        if nxt < 16:
                            wp_use[(tg * 8 + ho) % 2] = s["peC"].n
                            load_wp(g, nxt // 8, nxt % 8)

                    @blk.vector
                    def _(v, tg=tg, ho=ho, peC0=peC0):
                        for tt in range(4):
                            v.wait_ge(s["peC"].h, peC0 + tt + 1)
                            src = PS4[:, tt * 512:tt * 512 + 256]
                            ins = v.tensor_copy(
                                z_tg[:, tt * H + ho * 256:tt * H + (ho + 1) * 256], src)
                            proj_evac[(tg, ho, tt)] = s["vC"].inc(ins)

            proj_done[tg] = s["peC"].n
            sC_l1 = s["sC"].n
            vC0s_l1 = []
            if True:
                @blk.vector
                def _(v, tg=tg):
                    for tt in range(2):
                        zt = z_tg[:, tt * H:(tt + 1) * H]
                        fs, fv = xr_fill[(tg, tt)]
                        v.wait_ge(fs.h, fv)
                        for hoq in range(8):
                            q0 = proj_evac.get((tg, hoq, tt))
                        v.wait_ge(s["vC"].h, proj_evac[(tg, 7, tt)])
                        s["vC"].inc(v.tensor_tensor(zt, zt, xr2[tt % 2][:, :], ALU.add))
                        xr_use[(tg, tt + 2)] = s["vC"].n

                @blk.scalar
                def _(sc, tg=tg):
                    for tt in (2, 3):
                        sl = tt % 2
                        sc.wait_ge(s["vC"].h, xr_use[(tg, tt)])
                        sem = s["xr%d" % sl]
                        sem.inc(sc.dma_start(
                            out=xr2[sl][:, :],
                            in_=xrows[tg * 512 + tt * 128:tg * 512 + (tt + 1) * 128, :]), 16)
                        xr_fill[(tg, tt)] = (sem, sem.n)

                @blk.vector
                def _(v, tg=tg):
                    for tt in (2, 3):
                        zt = z_tg[:, tt * H:(tt + 1) * H]
                        fs, fv = xr_fill[(tg, tt)]
                        v.wait_ge(fs.h, fv)
                        v.wait_ge(s["vC"].h, proj_evac[(tg, 7, tt)])
                        s["vC"].inc(v.tensor_tensor(zt, zt, xr2[tt % 2][:, :], ALU.add))
                        xr_use[(tg + 1, tt - 2)] = s["vC"].n

            for tt in range(4):
                vC0 = s["vC"].n
                vC0s_l1.append(vC0)
                sC0 = sC_l1 + 2 * tt
                if True:
                    @blk.vector
                    def _(v, tt=tt, vC0=vC0, sC0=sC0):
                        zt = z_tg[:, tt * H:(tt + 1) * H]
                        _ln_vector(nc, s, v, zt, zt, vC0, sC0, stats, bnst)
                        v.wait_ge(s["vC"].h, s["vC"].n)
                        s["vC"].inc(v.tensor_copy(a1bf[:, :], zt))

                vC0b = s["vC"].n
                peC0 = s["peC"].n
                if True:
                    @blk.tensor
                    def _(t, tt=tt, vC0b=vC0b):
                        t.wait_ge(s["vC"].h, vC0b)
                        for half in range(2):
                            if half in pt_use:
                                t.wait_ge(s["vC"].h, pt_use[half])
                            for j in range(8):
                                ch = half * 8 + j
                                ins = t.transpose(PT[:, half * 1024 + j * 128:half * 1024 + (j + 1) * 128],
                                                  a1bf[:, ch * 128:(ch + 1) * 128], ident_sb[:, :])
                                if j == 7:
                                    s["peC"].inc(ins)

                    @blk.vector
                    def _(v, tt=tt, peC0=peC0):
                        for half in range(2):
                            v.wait_ge(s["peC"].h, peC0 + half + 1)
                            src = PT[:, half * 1024:(half + 1) * 1024].rearrange("p (j d) -> p j d", d=128)
                            dst = a1T.rearrange("p (ch c) -> p ch c", c=512)[:, half * 8:(half + 1) * 8,
                                                                            tt * 128:(tt + 1) * 128]
                            pt_use[half] = s["vC"].inc(v.tensor_copy(dst, src))

        if True:
            @blk.scalar
            def _(sc, vC0s=tuple(vC0s_l1), sC_l1=sC_l1):
                for tt in range(4):
                    zt = z_tg[:, tt * H:(tt + 1) * H]
                    _ln_scalar(nc, s, sc, zt, vC0s[tt], sC_l1 + 2 * tt, stats, consts)

        if tg == 0 and "d_a1" in dbg_h:
            if True:
                @blk.scalar
                def _(sc, v1=s["vC"].n):
                    sc.wait_ge(s["vC"].h, v1)
                    s["dbg"].inc(sc.dma_start(out=dbg_h["d_a1"][:, :], in_=z_tg[:, :]), 16)
                    s["dbg"].inc(sc.dma_start(out=dbg_h["d_a1T"][:, :], in_=a1T[:, :]), 16)

        cstk.close()

        if True:
            a1T_ready = s["vC"].n
            peC_f = s["peC"].n
            sC_f = s["sC"].n
            vC_f = s["vC"].n
            if True:
                @blk.gpsimd
                def _(g, tg=tg, peC_f=peC_f):
                    for it in range(2, NI):
                        gu_use[it % 2] = peC_f + (it - 1) * 2
                        load_gu(g, tg, it)
                    if tg == 0:
                        for it2 in range(2):
                            gu_use[it2 % 2] = peC_f + (NI - 1 + it2) * 2
                            load_gu(g, 1, it2)

                @blk.tensor
                def _(t, tg=tg, a1T_ready=a1T_ready, vC_f=vC_f, sC_f=sC_f):
                    for it in range(NI):
                        fs, fv = gu_fill[(tg, it)]
                        t.wait_ge(fs.h, fv)
                        if it == 0:
                            t.wait_ge(s["vC"].h, a1T_ready)
                        if it >= 2:
                            t.wait_ge(s["vC"].h, vC_f + it - 1)
                            t.wait_ge(s["sC"].h, sC_f + it - 1)
                        base = (it % 2) * 1024
                        for which, w_sb, off in (("g", gw_sb, 0), ("u", uw_sb, 512)):
                            for ch in range(NCH):
                                ins = t.matmul(PS4[:, base + off:base + off + 512],
                                               w_sb[it % 2][:, ch * 128:(ch + 1) * 128],
                                               a1T[:, ch * 512:(ch + 1) * 512],
                                               start=(ch == 0), stop=(ch == NCH - 1))
                                if ch == NCH - 1:
                                    s["peC"].inc(ins)

                @blk.scalar
                def _(sc, peC_f=peC_f, vC_f=vC_f):
                    for it in range(NI):
                        sc.wait_ge(s["vC"].h, vC_f + it)
                        sc.wait_ge(s["peC"].h, peC_f + 2 * it + 1)
                        ins = sc.activation(out=sg_bf[:, :],
                                            in_=PS4[:, (it % 2) * 1024:(it % 2) * 1024 + 512],
                                            func=AF.Silu, scale=gs_sb[:, it:it + 1])
                        s["sC"].inc(ins)

                @blk.vector
                def _(v, peC_f=peC_f, sC_f=sC_f):
                    for it in range(NI):
                        v.wait_ge(s["peC"].h, peC_f + 2 * it + 2)
                        v.wait_ge(s["sC"].h, sC_f + it + 1)
                        ins = v.scalar_tensor_tensor(
                            out=hdnT[:, it * 512:(it + 1) * 512],
                            in0=PS4[:, (it % 2) * 1024 + 512:(it % 2) * 1024 + 1024],
                            scalar=us_sb[:, it:it + 1], in1=sg_bf[:, :],
                            op0=ALU.mult, op1=ALU.mult)
                        s["vC"].inc(ins)

            hdn_all = s["vC"].n
            if tg == 0 and "d_hdnT" in dbg_h:
                if True:
                    @blk.scalar
                    def _(sc, hdn_all=hdn_all):
                        sc.wait_ge(s["vC"].h, hdn_all)
                        s["dbg"].inc(sc.dma_start(out=dbg_h["d_hdnT"][:, :], in_=hdnT[:, :]), 16)
            peC_d = s["peC"].n
            vC_d = s["vC"].n

            def _dcnt(nn):
                hh, jt = nn // 11, nn % 11
                return peC_d + hh * 14 + (jt + 1 if jt < 10 else 14)

            if True:
                @blk.gpsimd
                def _(g, tg=tg):
                    for nn in range(44):
                        if nn >= 2:
                            dw_use[nn % 2] = _dcnt(nn - 2)
                        load_dw(g, tg, (nn // 11), (nn % 11))
                    dw_use[0] = _dcnt(42)
                    dw_use[1] = _dcnt(43)

                @blk.tensor
                def _(t, tg=tg, hdn_all=hdn_all, vC_d=vC_d):
                    for hh in range(4):
                        for jt in range(11):
                            fs, fv = dw_fill[(tg, hh, jt)]
                            t.wait_ge(fs.h, fv)
                            if hh == 0 and jt == 0:
                                t.wait_ge(s["vC"].h, hdn_all)
                            if jt == 0 and hh > 0:
                                t.wait_ge(s["vC"].h, vC_d + (hh - 1) * 8 + 8)
                            for itl in range(4):
                                it = jt * 4 + itl
                                for tt in range(4):
                                    dst = PS4[:, tt * 512:(tt + 1) * 512]
                                    ins = t.matmul(dst,
                                                   hdnT[:, it * 512 + tt * 128:it * 512 + (tt + 1) * 128],
                                                   dw_sb[(hh * 11 + jt) % 2][:, itl * 512:(itl + 1) * 512],
                                                   start=(it == 0), stop=(it == NI - 1))
                                    if it == NI - 1:
                                        s["peC"].inc(ins)
                                    elif itl == 3 and tt == 3:
                                        s["peC"].inc(ins)

                @blk.vector
                def _(v, tg=tg):
                    if tg == 1:
                        v.wait_ge(s["out0"].h, 64)
                    for hh in range(4):
                        v.wait_ge(s["peC"].h, _dcnt(hh * 11 + 10))
                        for tt in range(4):
                            src_ = PS4[:, tt * 512:(tt + 1) * 512]
                            t2 = out_sb[0][:, hh * 512:(hh + 1) * 512]
                            v.wait_ge(s["vC"].h, s["vC"].n)
                            s["vC"].inc(v.tensor_tensor(t2, src_,
                                                        dsb_sb[:, hh * 512:(hh + 1) * 512], ALU.mult))
                            v.wait_ge(s["vC"].h, s["vC"].n)
                            ins = v.scalar_tensor_tensor(
                                out=z_tg[:, tt * H + hh * 512:tt * H + (hh + 1) * 512],
                                in0=t2, scalar=1.0,
                                in1=z_tg[:, tt * H + hh * 512:tt * H + (hh + 1) * 512],
                                op0=ALU.mult, op1=ALU.add)
                            s["vC"].inc(ins)

            if tg == 0 and "d_z2" in dbg_h:
                if True:
                    @blk.scalar
                    def _(sc, v1=s["vC"].n):
                        sc.wait_ge(s["vC"].h, v1)
                        s["dbg"].inc(sc.dma_start(out=dbg_h["d_z2"][:, :], in_=z_tg[:, :]), 16)

            sC_l2 = s["sC"].n
            vC0s_l2 = []
            apply_vals = []
            for tt in range(4):
                vC0 = s["vC"].n
                vC0s_l2.append(vC0)
                sC0 = sC_l2 + 2 * tt
                if True:
                    @blk.vector
                    def _(v, tg=tg, tt=tt, vC0=vC0, sC0=sC0):
                        zt = z_tg[:, tt * H:(tt + 1) * H]
                        prev_uses = 4 * tg + tt
                        if prev_uses > 0:
                            v.wait_ge(s["out0"].h, 16 * prev_uses)
                        _ln_vector(nc, s, v, zt, out_sb[tt % 2][:, :], vC0, sC0, stats, bnst)
                apply_vals.append(s["vC"].n)

            if True:
                @blk.scalar
                def _(sc, tg=tg, vC0s=tuple(vC0s_l2), sC_l2=sC_l2, applies=tuple(apply_vals)):
                    for tt in range(4):
                        zt = z_tg[:, tt * H:(tt + 1) * H]
                        _ln_scalar(nc, s, sc, zt, vC0s[tt], sC_l2 + 2 * tt, stats, consts)
                        sc.wait_ge(s["vC"].h, applies[tt])
                        sem = s["out0"]
                        sem.inc(
                            sc.dma_start(out=y[tg * 512 + tt * 128:tg * 512 + (tt + 1) * 128, :],
                                         in_=out_sb[tt % 2][:, :]), 16)
                        out_dma[(tg, tt)] = (sem, sem.n)

        hstk.close()

    if True:
        @blk.sync
        def _(sy):
            sy.wait_ge(s["out0"].h, s["out0"].n)
            sy.wait_ge(s["out1"].h, s["out1"].n)
            if dbg_h:
                sy.wait_ge(s["dbg"].h, s["dbg"].n)

        @blk.gpsimd
        def _(g):
            g.wait_ge(s["out0"].h, s["out0"].n)
            g.wait_ge(s["out1"].h, s["out1"].n)

    cst.close()
    st.close()
    return nc, dbg_h


def _ln_vector(nc, s, v, zt, out_ap, vC0, sC0, stats, bnst):
    v.wait_ge(s["vC"].h, vC0)
    ins = None
    for k in range(4):
        ins = v.bn_stats(bnst[:, k * 6:(k + 1) * 6], zt[:, k * 512:(k + 1) * 512])
    s["vC"].inc(ins)
    v.wait_ge(s["vC"].h, s["vC"].n)
    s["vC"].inc(v.bn_aggr(stats[:, 0:2], bnst[:, 0:24]))
    v.wait_ge(s["sC"].h, sC0 + 2)
    s["vC"].inc(v.tensor_scalar(out_ap, zt, stats[:, 0:1], stats[:, 3:4],
                                ALU.subtract, ALU.mult))


def _ln_scalar(nc, s, sc, zt, vC0, sC0, stats, consts):
    sc.wait_ge(s["vC"].h, vC0 + 2)
    s["sC"].inc(sc.activation(out=stats[:, 2:3], in_=stats[:, 1:2],
                              func=AF.Ln, bias=consts[:, 1:2]))
    sc.wait_ge(s["sC"].h, s["sC"].n)
    s["sC"].inc(sc.activation(out=stats[:, 3:4], in_=stats[:, 2:3],
                              func=AF.Exp, scale=-0.5))


def _host_inputs(inputs):
    x = np.ascontiguousarray(np.asarray(inputs["L0_attn_query"], dtype=np.float32))
    Wqkv = np.asarray(inputs["L0_attn_Wqkv"], dtype=np.float32)
    Wproj = np.asarray(inputs["L0_attn_Wproj"], dtype=np.float32)
    bproj = np.asarray(inputs["L0_attn_bproj"], dtype=np.float32)
    gate_w = np.asarray(inputs["L0_ffn_gate_w"], dtype=np.float32)
    up_w = np.asarray(inputs["L0_ffn_up_w"], dtype=np.float32)
    down_w = np.asarray(inputs["L0_ffn_down_w"], dtype=np.float32)
    gate_s = np.asarray(inputs["L0_ffn_gate_s"], dtype=np.float32)
    up_s = np.asarray(inputs["L0_ffn_up_s"], dtype=np.float32)
    down_s = np.asarray(inputs["L0_ffn_down_s"], dtype=np.float32)

    WqkvT = np.ascontiguousarray(Wqkv.T)
    xTb = [np.ascontiguousarray(x[b].T) for b in range(2)]
    wpT = np.ascontiguousarray(Wproj.reshape(8, 256, 16, 128).transpose(0, 3, 2, 1).reshape(8 * 128, 4096))
    gwT = np.ascontiguousarray(gate_w.reshape(44, 128, 16, 128).transpose(0, 3, 2, 1).reshape(44 * 128, 2048))
    uwT = np.ascontiguousarray(up_w.reshape(44, 128, 16, 128).transpose(0, 3, 2, 1).reshape(44 * 128, 2048))
    dwT = np.ascontiguousarray(down_w.reshape(4, 512, 11, 4, 128).transpose(0, 2, 4, 3, 1).reshape(44 * 128, 2048))
    gs2d = np.ascontiguousarray(gate_s.reshape(NI, 128).T)
    us2d = np.ascontiguousarray(up_s.reshape(NI, 128).T)
    dsb = np.ascontiguousarray(np.broadcast_to(down_s, (128, H)))
    bias2d = np.ascontiguousarray(bproj.reshape(1, H))
    ones_bf = np.ones((1, 128), dtype=ml_dtypes.bfloat16)
    ident_bf = np.eye(128, dtype=ml_dtypes.bfloat16)

    in_maps = []
    for c in range(8):
        beta = (c % 4) // 2
        par = 1 if c >= 4 else 0
        heads = [8 * (c % 2) + 2 * gi + par for gi in range(4)]
        q = c % 4

        def slab(base):
            cols = [WqkvT[:, h * 384 + base:h * 384 + base + 128] for h in heads]
            return np.ascontiguousarray(np.concatenate(cols, axis=1))

        idx = np.zeros((64, 1), dtype=np.uint32)
        for p in range(64):
            rank, gi, sub = p // 16, (p // 4) % 4, p % 4
            idx[p, 0] = rank * 128 + gi * 32 + q * 8 + sub * 2
        bp = c // 4
        in_maps.append({
            "xT": xTb[beta],
            "wq": slab(0), "wk": slab(128), "wv": slab(256),
            "wpT": wpT, "bias": bias2d,
            "gwT": gwT, "uwT": uwT, "dwT": dwT,
            "gs2d": gs2d, "us2d": us2d, "dsb_in": dsb,
            "xrows": np.ascontiguousarray(x[bp, q * 1024:(q + 1) * 1024, :]),
            "gidx": idx,
            "ones_in": ones_bf, "ident_in": ident_bf,
        })
    return in_maps


def kernel(**inputs):
    in_maps = _host_inputs(inputs)
    nc, _ = build_nc()
    res = run_bass_kernel_spmd(nc, in_maps, core_ids=list(range(8)))
    out = np.empty((2, 4096, 2048), dtype=np.float32)
    for c in range(8):
        bp, q = c // 4, c % 4
        out[bp, q * 1024:(q + 1) * 1024, :] = np.asarray(res.results[c]["y"])
    return out


if __name__ == "__main__":
    nc, _ = build_nc()
    print("built ok, instructions:", nc.n_instructions() if callable(getattr(nc, "n_instructions", None)) else "?")


# version 12
# speedup vs baseline: 1.1299x; 1.1242x over previous
import contextlib
import numpy as np
import ml_dtypes
import concourse.bass as bass
import concourse.mybir as mybir
from concourse.bass_utils import run_bass_kernel_spmd

F32 = mybir.dt.float32
BF16 = mybir.dt.bfloat16
U32 = mybir.dt.uint32
AF = mybir.ActivationFunctionType
ALU = mybir.AluOpType
AX = mybir.AxisListType

H = 2048
SQ = 4096
NCH = 16
ITER = 5632
NI = 44
SCALE = 0.08838834764831845
EXPOFF = -10.0
EPS = 1e-5
NT = 1024
IDT = 1.0 / 2048.0


class Sem:
    def __init__(self, st, nc, name):
        self.h = st.enter_context(nc.semaphore(name))
        self.n = 0

    def inc(self, instr, k=1):
        instr.then_inc(self.h, k)
        self.n += k
        return self.n


def build_nc(dbg=()):
    nc = bass.Bass("TRN2", target_bir_lowering=False, num_devices=8)

    xT = nc.dram_tensor("xT", [H, SQ], F32, kind="ExternalInput").ap()
    wq = nc.dram_tensor("wq", [H, 512], F32, kind="ExternalInput").ap()
    wk = nc.dram_tensor("wk", [H, 512], F32, kind="ExternalInput").ap()
    wv = nc.dram_tensor("wv", [H, 512], F32, kind="ExternalInput").ap()
    wpT = nc.dram_tensor("wpT", [8 * 128, 4096], F32, kind="ExternalInput").ap()
    bias = nc.dram_tensor("bias", [1, H], F32, kind="ExternalInput").ap()
    gwT = nc.dram_tensor("gwT", [NI * 128, 2048], F32, kind="ExternalInput").ap()
    uwT = nc.dram_tensor("uwT", [NI * 128, 2048], F32, kind="ExternalInput").ap()
    dwT = nc.dram_tensor("dwT", [NI * 128, 2048], F32, kind="ExternalInput").ap()
    gs2d = nc.dram_tensor("gs2d", [128, NI], F32, kind="ExternalInput").ap()
    us2d = nc.dram_tensor("us2d", [128, NI], F32, kind="ExternalInput").ap()
    dsb_in = nc.dram_tensor("dsb_in", [128, H], F32, kind="ExternalInput").ap()
    xrows = nc.dram_tensor("xrows", [NT, H], F32, kind="ExternalInput").ap()
    gidx = nc.dram_tensor("gidx", [64, 1], U32, kind="ExternalInput").ap()
    ones_in = nc.dram_tensor("ones_in", [1, 128], BF16, kind="ExternalInput").ap()
    ident_in = nc.dram_tensor("ident_in", [128, 128], BF16, kind="ExternalInput").ap()
    y = nc.dram_tensor("y", [NT, H], F32, kind="ExternalOutput").ap()

    gw_bf = nc.dram_tensor("gw_bf", [NI * 128, 2048], BF16, kind="Internal").ap()
    uw_bf = nc.dram_tensor("uw_bf", [NI * 128, 2048], BF16, kind="Internal").ap()
    dw_bf = nc.dram_tensor("dw_bf", [4 * 11 * 128, 2048], BF16, kind="Internal").ap()
    wp_bf = nc.dram_tensor("wp_bf", [8 * 128, 4096], BF16, kind="Internal").ap()
    ag_in = nc.dram_tensor("ag_in", [4 * SQ, 128], BF16, kind="Internal").ap()
    ag_out = nc.dram_tensor("ag_out", [16 * SQ, 128], BF16, kind="Internal").ap()
    ctx_loc = nc.dram_tensor("ctx_loc", [16 * 512, 128], BF16, kind="Internal").ap()

    dbg_h = {}
    for name, shape, dt in dbg:
        dbg_h[name] = nc.dram_tensor(name, shape, dt, kind="ExternalOutput").ap()

    st = contextlib.ExitStack()

    consts = st.enter_context(nc.sbuf_tensor("consts", [128, 8], F32))
    stage2 = [st.enter_context(nc.sbuf_tensor("stage%d" % i, [128, 512], BF16)) for i in range(2)]
    ones_sb = st.enter_context(nc.sbuf_tensor("ones_sb", [1, 128], BF16))
    ident_sb = st.enter_context(nc.sbuf_tensor("ident_sb", [128, 128], BF16))
    bias_sb = st.enter_context(nc.sbuf_tensor("bias_sb", [1, H], BF16, side="right"))
    gs_sb = st.enter_context(nc.sbuf_tensor("gs_sb", [128, NI], F32))
    us_sb = st.enter_context(nc.sbuf_tensor("us_sb", [128, NI], F32))
    idx_sb = st.enter_context(nc.sbuf_tensor("idx_sb", [64, 1], U32))
    stats = st.enter_context(nc.sbuf_tensor("stats", [128, 8], F32))
    rs_sb = st.enter_context(nc.sbuf_tensor("rs_sb", [128, 4], F32))

    PS4 = st.enter_context(nc.psum_tensor("PS4", [128, 2048], F32))
    PB2 = st.enter_context(nc.psum_tensor("PB2", [128, 1024], F32))
    PT = st.enter_context(nc.psum_tensor("PTp", [128, 2048], BF16))

    sems = {}
    for name in ["wA", "xb0", "xb1", "xf", "peA", "evA", "peB", "exp", "peB2", "rec",
                 "cts", "stg", "cc", "g", "loc", "ct", "wp0", "wp1", "gu0", "gu1",
                 "dw0", "dw1", "xr", "peC", "vC", "sC", "out0", "out1",
                 "pc0", "pc1", "pc2", "pc3", "xr0", "xr1", "pcast", "stgA", "stgB", "dbg"]:
        sems[name] = Sem(st, nc, "s_" + name)
    s = sems
    blk = st.enter_context(nc.Block())

    abst = contextlib.ExitStack()
    qT = abst.enter_context(nc.sbuf_tensor("qT", [128, 4 * SQ], BF16))
    kT = abst.enter_context(nc.sbuf_tensor("kT", [128, 4 * SQ], BF16))
    VA = abst.enter_context(nc.sbuf_tensor("VA", [128, 4 * 32 * 129], BF16))

    gath0 = st.enter_context(nc.sbuf_tensor("gath0", [64, 8192], BF16, side="right"))

    ast = contextlib.ExitStack()
    wq_sb = ast.enter_context(nc.sbuf_tensor("wq_sb", [128, NCH * 512], BF16, side="right"))
    wk_sb = ast.enter_context(nc.sbuf_tensor("wk_sb", [128, NCH * 512], BF16, side="right"))
    wv_sb = ast.enter_context(nc.sbuf_tensor("wv_sb", [128, NCH * 512], BF16, side="right"))
    xbuf = [ast.enter_context(nc.sbuf_tensor(f"xbuf{i}", [128, NCH * 512], BF16, side="right"))
            for i in range(2)]

    if True:
        @blk.gpsimd
        def _(g):
            for w_in, w_sb in ((wq, wq_sb), (wk, wk_sb), (wv, wv_sb)):
                src = w_in.rearrange("(ch p) c -> p ch c", p=128)
                dst = w_sb.rearrange("p (ch c) -> p ch c", c=512)
                s["wA"].inc(g.dma_start(out=dst, in_=src), 16)
            s["wA"].inc(g.dma_start(out=bias_sb[:, :], in_=bias[:, :]), 16)

        @blk.sync
        def _(sy):
            s["wA"].inc(sy.dma_start(out=ones_sb[:, :], in_=ones_in[:, :]), 16)
            s["wA"].inc(sy.dma_start(out=ident_sb[:, :], in_=ident_in[:, :]), 16)
            s["wA"].inc(sy.dma_start(out=gs_sb[:, :], in_=gs2d[:, :]), 16)
            s["wA"].inc(sy.dma_start(out=us_sb[:, :], in_=us2d[:, :]), 16)
            s["wA"].inc(sy.dma_start(out=idx_sb[:, :], in_=gidx[:, :]), 16)

        @blk.vector
        def _(v):
            v.memset(consts[:, 0:1], EXPOFF)
            v.memset(consts[:, 1:2], EPS)
            s["evA"].inc(v.memset(VA[:, :], 1.0))

    evA0 = s["evA"].n

    pc_jobs = []
    for w_in, w_out, rows in ((wpT, wp_bf, 8 * 128), (gwT, gw_bf, NI * 128),
                              (uwT, uw_bf, NI * 128), (dwT, dw_bf, NI * 128)):
        step = 512
        for r0 in range(0, rows, step):
            pc_jobs.append((w_in[r0:min(r0 + step, rows), :],
                            w_out[r0:min(r0 + step, rows), :]))
    PCAST_TOTAL = len(pc_jobs) * 16

    def emit_precast(g, lo, hi):
        for j in range(lo, min(hi, len(pc_jobs))):
            src_, dst_ = pc_jobs[j]
            s["pcast"].inc(g.dma_start(out=dst_, in_=src_), 16)

    for sg in range(8):
        if True:
            @blk.gpsimd
            def _(g, sg=sg):
                if sg >= 2:
                    g.wait_ge(s["evA"].h, evA0 + (sg - 2) * 12 + 12)
                src = xT[:, sg * 512:(sg + 1) * 512].rearrange("(ch p) c -> p ch c", p=128)
                dst = xbuf[sg % 2].rearrange("p (ch c) -> p ch c", c=512)
                s["xb%d" % (sg % 2)].inc(g.dma_start(out=dst, in_=src), 16)
                emit_precast(g, sg * 2, sg * 2 + 2)

            @blk.tensor
            def _(t, sg=sg):
                t.wait_ge(s["xb%d" % (sg % 2)].h, (sg // 2 + 1) * 16)
                if sg == 0:
                    t.wait_ge(s["wA"].h, s["wA"].n)
                xb = xbuf[sg % 2]
                peA0 = 12 * sg
                for half in range(2):
                    if half == 0:
                        if sg > 0:
                            t.wait_ge(s["evA"].h, evA0 + (sg - 1) * 12 + 8)
                    else:
                        t.wait_ge(s["evA"].h, evA0 + sg * 12 + 4)
                    for qk, w_sb in ((0, wq_sb), (1, wk_sb)):
                        for gl in range(2):
                            gi = half * 2 + gl
                            dst = PS4[:, (qk * 2 + gl) * 512:(qk * 2 + gl + 1) * 512]
                            for ch in range(NCH):
                                ins = t.matmul(dst,
                                               w_sb[:, ch * 512 + gi * 128:ch * 512 + (gi + 1) * 128],
                                               xb[:, ch * 512:(ch + 1) * 512],
                                               start=(ch == 0), stop=(ch == NCH - 1))
                                if ch == NCH - 1:
                                    s["peA"].inc(ins)
                for stt in range(4):
                    if stt < 2:
                        if sg > 0:
                            t.wait_ge(s["evA"].h, evA0 + (sg - 1) * 12 + 11 + stt)
                    else:
                        t.wait_ge(s["evA"].h, evA0 + sg * 12 + 7 + stt)
                    dst = PB2[:, (stt % 2) * 512:(stt % 2 + 1) * 512]
                    for ch in range(NCH):
                        ins = t.matmul(dst,
                                       xb[:, ch * 512 + stt * 128:ch * 512 + (stt + 1) * 128],
                                       wv_sb[:, ch * 512:(ch + 1) * 512],
                                       start=(ch == 0), stop=(ch == NCH - 1))
                        if ch == NCH - 1:
                            s["peA"].inc(ins)

            @blk.vector
            def _(v, sg=sg):
                peA0 = 12 * sg
                order = [("q", 0, 0), ("q", 1, 1), ("k", 0, 2), ("k", 1, 3),
                         ("q", 2, 0), ("q", 3, 1), ("k", 2, 2), ("k", 3, 3)]
                for i, (kind, gi, qtr) in enumerate(order):
                    v.wait_ge(s["peA"].h, peA0 + i + 1)
                    dstT = qT if kind == "q" else kT
                    s["evA"].inc(v.tensor_copy(dstT[:, gi * SQ + sg * 512:gi * SQ + (sg + 1) * 512],
                                               PS4[:, qtr * 512:(qtr + 1) * 512]))
                for stt in range(4):
                    v.wait_ge(s["peA"].h, peA0 + 9 + stt)
                    kc = sg * 4 + stt
                    src = PB2[:, (stt % 2) * 512:(stt % 2 + 1) * 512]
                    ins = None
                    for gi in range(4):
                        ins = v.tensor_copy(VA[:, gi * 32 * 129 + kc * 129:gi * 32 * 129 + kc * 129 + 128],
                                            src[:, gi * 128:(gi + 1) * 128])
                    s["evA"].inc(ins)

    if "d_qT" in dbg_h:
        if True:
            @blk.sync
            def _(sy):
                sy.wait_ge(s["evA"].h, evA0 + 96)
                s["dbg"].inc(sy.dma_start(out=dbg_h["d_qT"][:, :], in_=qT[:, :]), 16)
                s["dbg"].inc(sy.dma_start(out=dbg_h["d_kT"][:, :], in_=kT[:, :]), 16)
                s["dbg"].inc(sy.dma_start(out=dbg_h["d_VA"][:, :], in_=VA[:, :]), 16)

    ast.close()

    bst = contextlib.ExitStack()
    PTb2 = [bst.enter_context(nc.sbuf_tensor("PTb%d" % i, [128, 32 * 512], BF16, side="right"))
            for i in range(2)]

    QORDER = [(gi, qg) for half in (0, 1) for gi in range(4) for qg in range(half, 8, 2)]
    if True:
        @blk.tensor
        def _(t):
            t.wait_ge(s["evA"].h, evA0 + 96)

            def emit_s(QG, kts):
                gi, qg = QORDER[QG]
                exp0 = QG * 16
                for kt in kts:
                    if kt >= 4:
                        w = exp0 + (kt - 4) // 2 + 1
                    else:
                        w = exp0 + (-1, -1, 0, 0)[kt]
                    if w > 0:
                        t.wait_ge(s["exp"].h, w)
                    ins = t.matmul(PS4[:, (kt % 4) * 512:(kt % 4 + 1) * 512],
                                   kT[:, gi * SQ + kt * 128:gi * SQ + (kt + 1) * 128],
                                   qT[:, gi * SQ + qg * 512:gi * SQ + (qg + 1) * 512],
                                   start=True, stop=True)
                    s["peB"].inc(ins)

            emit_s(0, range(32))
            for QG in range(32):
                gi, qg = QORDER[QG]
                exp0 = QG * 16
                cts0 = QG * 4
                for qq in range(4):
                    w = cts0 + qq - 1
                    if w > 0:
                        t.wait_ge(s["cts"].h, w)
                    dst = PB2[:, (qq % 2) * 512:(qq % 2) * 512 + 129]
                    for kc in range(32):
                        if kc % 8 == 0:
                            t.wait_ge(s["exp"].h, exp0 + min(16, kc // 2 + 4))
                        ins = t.matmul(dst,
                                       PTb2[QG % 2][:, kc * 512 + qq * 128:kc * 512 + (qq + 1) * 128],
                                       VA[:, gi * 32 * 129 + kc * 129:gi * 32 * 129 + (kc + 1) * 129],
                                       start=(kc == 0), stop=(kc == 31))
                        if kc == 31:
                            s["peB2"].inc(ins)
                    if QG + 1 < 32:
                        emit_s(QG + 1, range(qq * 8, qq * 8 + 8))

        @blk.scalar
        def _(sc):
            sc.wait_ge(s["evA"].h, evA0)
            for QG in range(32):
                peB0 = QG * 32
                guard = (QG - 1) * 4
                if guard > 0:
                    sc.wait_ge(s["peB2"].h, guard)
                for j in range(16):
                    sc.wait_ge(s["peB"].h, peB0 + 2 * j + 2)
                    bk = (2 * j) % 4
                    ins = sc.activation(out=PTb2[QG % 2][:, 2 * j * 512:(2 * j + 2) * 512],
                                        in_=PS4[:, bk * 512:(bk + 2) * 512],
                                        func=AF.Exp, bias=consts[:, 0:1], scale=SCALE)
                    s["exp"].inc(ins)

        @blk.vector
        def _(v):
            for QG in range(32):
                peB20 = QG * 4
                if QG > 1:
                    v.wait_ge(s["stg%s" % "AB"[QG % 2]].h, (QG // 2) * 16)
                stg_buf = stage2[QG % 2]
                for qq in range(4):
                    v.wait_ge(s["peB2"].h, peB20 + qq + 1)
                    src_ = PB2[:, (qq % 2) * 512:(qq % 2) * 512 + 129]
                    s["rec"].inc(v.reciprocal(rs_sb[:, qq:qq + 1], src_[:, 128:129]))
                    v.wait_ge(s["rec"].h, s["rec"].n)
                    ins = v.tensor_scalar(stg_buf[:, qq * 128:(qq + 1) * 128],
                                          src_[:, 0:128], rs_sb[:, qq:qq + 1], None, ALU.mult)
                    s["cts"].inc(ins)

        @blk.sync
        def _(sy):
            for QG in range(32):
                gi, qg = QORDER[QG]
                half, q4 = qg % 2, qg // 2
                base = half * 8192 + gi * 2048 + q4 * 512
                sy.wait_ge(s["cts"].h, QG * 4 + 4)
                dst = ag_in[base:base + 512, :].rearrange("(q p) d -> p q d", p=128)
                src_ = stage2[QG % 2].rearrange("p (q d) -> p q d", d=128)
                ins = sy.dma_start(out=dst, in_=src_)
                s["stg%s" % "AB"[QG % 2]].inc(ins, 16)

    if "d_agin" in dbg_h:
        if True:
            @blk.sync
            def _(sy):
                sy.wait_ge(s["stg"].h, s["stg"].n)
                s["dbg"].inc(sy.dma_start(out=dbg_h["d_agin"][:, :], in_=ag_in[:, :]), 16)

    bst.close()

    gw_sb = [st.enter_context(nc.sbuf_tensor(f"gw{i}", [128, NCH * 128], BF16, side="right")) for i in range(2)]
    uw_sb = [st.enter_context(nc.sbuf_tensor(f"uw{i}", [128, NCH * 128], BF16, side="right")) for i in range(2)]
    dw_sb = [st.enter_context(nc.sbuf_tensor(f"dw{i}", [128, 4 * 512], BF16, side="right")) for i in range(2)]
    wp_sb = [st.enter_context(nc.sbuf_tensor(f"wp{i}", [128, NCH * 256], BF16, side="right")) for i in range(2)]
    dsb_sb = st.enter_context(nc.sbuf_tensor("dsb_sb", [128, H], F32, side="right"))

    wp_fill = {}
    gu_fill = {}
    dw_fill = {}
    wp_use = {}
    gu_use = {}
    dw_use = {}

    pcast_waited = [False]

    def _pcast_gate(g):
        if not pcast_waited[0]:
            g.wait_ge(s["pcast"].h, PCAST_TOTAL)
            pcast_waited[0] = True

    def load_wp(g, tg, ho):
        slot = (tg * 8 + ho) % 2
        if slot in wp_use:
            g.wait_ge(s["peC"].h, wp_use[slot])
        sem = s["wp%d" % slot]
        sem.inc(g.dma_start(out=wp_sb[slot][:, :], in_=wp_bf[ho * 128:(ho + 1) * 128, :]), 16)
        wp_fill[(tg, ho)] = (sem, sem.n)

    def load_gu(g, tg, it):
        slot = it % 2
        if slot in gu_use:
            g.wait_ge(s["peC"].h, gu_use[slot])
        sem = s["gu%d" % slot]
        sem.inc(g.dma_start(out=gw_sb[slot][:, :], in_=gw_bf[it * 128:(it + 1) * 128, :]), 16)
        sem.inc(g.dma_start(out=uw_sb[slot][:, :], in_=uw_bf[it * 128:(it + 1) * 128, :]), 16)
        gu_fill[(tg, it)] = (sem, sem.n)

    def load_dw(g, tg, hh, jt):
        slot = (hh * 11 + jt) % 2
        if slot in dw_use:
            g.wait_ge(s["peC"].h, dw_use[slot])
        sem = s["dw%d" % slot]
        sem.inc(g.dma_start(out=dw_sb[slot][:, :],
                            in_=dw_bf[(hh * 11 + jt) * 128:(hh * 11 + jt + 1) * 128, :]), 16)
        dw_fill[(tg, hh, jt)] = (sem, sem.n)

    if True:
        @blk.gpsimd
        def _(g):
            for j in range(16, min(26, len(pc_jobs))):
                g.wait_ge(s["exp"].h, (j - 16) * 26)
                src_, dst_ = pc_jobs[j]
                s["pcast"].inc(g.dma_start(out=dst_, in_=src_), 16)
            g.wait_ge(s["stgA"].h, 128)
            g.wait_ge(s["stgB"].h, 128)
            ins = g.collective_compute(
                "AllGather", ALU.bypass,
                replica_groups=[[0, 1, 2, 3], [4, 5, 6, 7]],
                ins=[ag_in[0:8192, :].rearrange("a b -> (a b)")[None, :]],
                outs=[ag_out[0:32768, :].rearrange("a b -> (a b)")[None, :]],
            )
            s["cc"].inc(ins)
            for j in range(26, len(pc_jobs)):
                g.wait_ge(s["exp"].h, min(500, 280 + (j - 26) * 24))
                src_, dst_ = pc_jobs[j]
                s["pcast"].inc(g.dma_start(out=dst_, in_=src_), 16)
            g.wait_ge(s["cc"].h, 1)
            for half in range(2):
                if half == 1:
                    g.wait_ge(s["loc"].h, 256)
                ins = g.indirect_dma_start(
                    out=gath0[:, :], out_offset=None,
                    in_=ag_out.rearrange("(c r) d -> c (r d)", r=64),
                    in_offset=bass.IndirectOffsetOnAxis(idx_sb[:, 0:1], 0),
                    element_offset=half * 8192,
                )
                s["g"].inc(ins, 16)
                g.wait_ge(s["g"].h, 16 * (half + 1))
                for hb in range(16):
                    src_ = gath0[hb * 4:(hb + 1) * 4, :].rearrange("q (j d) -> q j d", d=128)
                    dst_ = ctx_loc[hb * 512:(hb + 1) * 512, :] \
                        .rearrange("(q j) d -> q j d", j=128)[:, half * 64:(half + 1) * 64, :]
                    s["loc"].inc(g.dma_start(out=dst_, in_=src_), 16)
            g.wait_ge(s["stgA"].h, 256)
            g.wait_ge(s["stgB"].h, 256)
            ins = g.collective_compute(
                "AllGather", ALU.bypass,
                replica_groups=[[0, 1, 2, 3], [4, 5, 6, 7]],
                ins=[ag_in[8192:16384, :].rearrange("a b -> (a b)")[None, :]],
                outs=[ag_out[32768:65536, :].rearrange("a b -> (a b)")[None, :]],
            )
            s["cc"].inc(ins)
            _pcast_gate(g)
            load_wp(g, 0, 0)
            load_wp(g, 0, 1)
            load_gu(g, 0, 0)
            load_gu(g, 0, 1)

        @blk.sync
        def _(sy):
            s["wA"].inc(sy.dma_start(out=dsb_sb[:, :], in_=dsb_in[:, :]), 16)

    abst.close()
    cst = contextlib.ExitStack()
    z_tg = cst.enter_context(nc.sbuf_tensor("z_tg", [128, 4 * H], F32))
    xr2 = [cst.enter_context(nc.sbuf_tensor("xr2_%d" % i, [128, H], F32)) for i in range(2)]
    a1T = cst.enter_context(nc.sbuf_tensor("a1T", [128, NCH * 512], BF16))
    bnst = cst.enter_context(nc.sbuf_tensor("bnst", [128, 24], F32))
    out_sb = [cst.enter_context(nc.sbuf_tensor("out0", [128, H], F32))] * 2
    sg_bf = stage2[1]

    proj_evac = {}
    out_dma = {}
    pt_use = {}

    proj_done = {}
    xr_fill = {}
    xr_use = {}
    for tg in range(2):
        g0 = s["g"].n
        loc0 = s["loc"].n
        peC_j = s["peC"].n
        vC_j = s["vC"].n
        sC_j = s["sC"].n
        hstk = contextlib.ExitStack()
        hdnT = hstk.enter_context(nc.sbuf_tensor("hdnT%d" % tg, [128, NI * 512], BF16))
        if tg == 1:
            if True:
                @blk.gpsimd
                def _(g, tg=tg, loc0=loc0):
                    g.wait_ge(s["cc"].h, 2)
                    g.wait_ge(s["loc"].h, loc0)
                    for half in range(2):
                        if half == 1:
                            g.wait_ge(s["loc"].h, loc0 + 256)
                        ins = g.indirect_dma_start(
                            out=gath0[:, :], out_offset=None,
                            in_=ag_out.rearrange("(c r) d -> c (r d)", r=64),
                            in_offset=bass.IndirectOffsetOnAxis(idx_sb[:, 0:1], 0),
                            element_offset=tg * 512 * 8192 + half * 8192,
                        )
                        s["g"].inc(ins, 16)
                        g.wait_ge(s["g"].h, s["g"].n)
                        for hb in range(16):
                            src_ = gath0[hb * 4:(hb + 1) * 4, :].rearrange("q (j d) -> q j d", d=128)
                            dst_ = ctx_loc[hb * 512:(hb + 1) * 512, :] \
                                .rearrange("(q j) d -> q j d", j=128)[:, half * 64:(half + 1) * 64, :]
                            s["loc"].inc(g.dma_start(out=dst_, in_=src_), 16)

        cstk = contextlib.ExitStack()
        ctxT = cstk.enter_context(nc.sbuf_tensor("ctxT%d" % tg, [128, NCH * 512], BF16))
        a1bf = cstk.enter_context(nc.sbuf_tensor("a1bf%d" % tg, [128, H], BF16))
        if True:
            if True:
                @blk.sync
                def _(sy, tg=tg):
                    sy.wait_ge(s["loc"].h, s["loc"].n)
                    if tg == 1:
                        sy.wait_ge(s["peC"].h, proj_done[0])
                    for hb in range(16):
                        s["ct"].inc(sy.dma_start_transpose(out=ctxT[:, hb * 512:(hb + 1) * 512],
                                                           in_=ctx_loc[hb * 512:(hb + 1) * 512, :]), 16)

                @blk.scalar
                def _(sc, tg=tg):
                    for tt in range(2):
                        sl = tt % 2
                        if (tg, tt) in xr_use:
                            sc.wait_ge(s["vC"].h, xr_use[(tg, tt)])
                        sem = s["xr%d" % sl]
                        sem.inc(sc.dma_start(
                            out=xr2[sl][:, :],
                            in_=xrows[tg * 512 + tt * 128:tg * 512 + (tt + 1) * 128, :]), 16)
                        xr_fill[(tg, tt)] = (sem, sem.n)

            ct_done = s["ct"].n

            if tg == 0 and "d_ctxT" in dbg_h:
                if True:
                    @blk.scalar
                    def _(sc, ct_done=ct_done):
                        sc.wait_ge(s["ct"].h, ct_done)
                        s["dbg"].inc(sc.dma_start(out=dbg_h["d_ctxT"][:, :], in_=ctxT[:, :]), 16)

            for ho in range(8):
                peC0 = s["peC"].n
                if True:
                    @blk.tensor
                    def _(t, tg=tg, ho=ho, ct_done=ct_done):
                        fs, fv = wp_fill[(tg, ho)]
                        t.wait_ge(fs.h, fv)
                        if ho == 0:
                            t.wait_ge(s["ct"].h, ct_done)
                        for tt in range(4):
                            key = (tg, ho - 1, tt) if ho > 0 else (tg - 1, 7, tt)
                            if key in proj_evac:
                                t.wait_ge(s["vC"].h, proj_evac[key])
                            dst = PS4[:, tt * 512:tt * 512 + 256]
                            for ch in range(NCH):
                                t.matmul(dst,
                                         ctxT[:, ch * 512 + tt * 128:ch * 512 + (tt + 1) * 128],
                                         wp_sb[(tg * 8 + ho) % 2][:, ch * 256:(ch + 1) * 256],
                                         start=(ch == 0), stop=False)
                            ins = t.matmul(dst, ones_sb[0:1, :],
                                           bias_sb[0:1, ho * 256:(ho + 1) * 256],
                                           start=False, stop=True)
                            s["peC"].inc(ins)

                    @blk.gpsimd
                    def _(g, tg=tg, ho=ho):
                        nxt = tg * 8 + ho + 2
                        if nxt < 16:
                            wp_use[(tg * 8 + ho) % 2] = s["peC"].n
                            load_wp(g, nxt // 8, nxt % 8)

                    @blk.vector
                    def _(v, tg=tg, ho=ho, peC0=peC0):
                        for tt in range(4):
                            v.wait_ge(s["peC"].h, peC0 + tt + 1)
                            src = PS4[:, tt * 512:tt * 512 + 256]
                            ins = v.tensor_copy(
                                z_tg[:, tt * H + ho * 256:tt * H + (ho + 1) * 256], src)
                            proj_evac[(tg, ho, tt)] = s["vC"].inc(ins)

            proj_done[tg] = s["peC"].n
            sC_l1 = s["sC"].n
            vC0s_l1 = []
            if True:
                @blk.vector
                def _(v, tg=tg):
                    for tt in range(2):
                        zt = z_tg[:, tt * H:(tt + 1) * H]
                        fs, fv = xr_fill[(tg, tt)]
                        v.wait_ge(fs.h, fv)
                        for hoq in range(8):
                            q0 = proj_evac.get((tg, hoq, tt))
                        v.wait_ge(s["vC"].h, proj_evac[(tg, 7, tt)])
                        s["vC"].inc(v.tensor_tensor(zt, zt, xr2[tt % 2][:, :], ALU.add))
                        xr_use[(tg, tt + 2)] = s["vC"].n

                @blk.scalar
                def _(sc, tg=tg):
                    for tt in (2, 3):
                        sl = tt % 2
                        sc.wait_ge(s["vC"].h, xr_use[(tg, tt)])
                        sem = s["xr%d" % sl]
                        sem.inc(sc.dma_start(
                            out=xr2[sl][:, :],
                            in_=xrows[tg * 512 + tt * 128:tg * 512 + (tt + 1) * 128, :]), 16)
                        xr_fill[(tg, tt)] = (sem, sem.n)

                @blk.vector
                def _(v, tg=tg):
                    for tt in (2, 3):
                        zt = z_tg[:, tt * H:(tt + 1) * H]
                        fs, fv = xr_fill[(tg, tt)]
                        v.wait_ge(fs.h, fv)
                        v.wait_ge(s["vC"].h, proj_evac[(tg, 7, tt)])
                        s["vC"].inc(v.tensor_tensor(zt, zt, xr2[tt % 2][:, :], ALU.add))
                        xr_use[(tg + 1, tt - 2)] = s["vC"].n

            for tt in range(4):
                vC0 = s["vC"].n
                vC0s_l1.append(vC0)
                sC0 = sC_l1 + 2 * tt
                if True:
                    @blk.vector
                    def _(v, tt=tt, vC0=vC0, sC0=sC0):
                        zt = z_tg[:, tt * H:(tt + 1) * H]
                        _ln_vector(nc, s, v, zt, zt, vC0, sC0, stats, bnst)
                        v.wait_ge(s["vC"].h, s["vC"].n)
                        s["vC"].inc(v.tensor_copy(a1bf[:, :], zt))

                vC0b = s["vC"].n
                peC0 = s["peC"].n
                if True:
                    @blk.tensor
                    def _(t, tt=tt, vC0b=vC0b):
                        t.wait_ge(s["vC"].h, vC0b)
                        for half in range(2):
                            if half in pt_use:
                                t.wait_ge(s["vC"].h, pt_use[half])
                            for j in range(8):
                                ch = half * 8 + j
                                ins = t.transpose(PT[:, half * 1024 + j * 128:half * 1024 + (j + 1) * 128],
                                                  a1bf[:, ch * 128:(ch + 1) * 128], ident_sb[:, :])
                                if j == 7:
                                    s["peC"].inc(ins)

                    @blk.vector
                    def _(v, tt=tt, peC0=peC0):
                        for half in range(2):
                            v.wait_ge(s["peC"].h, peC0 + half + 1)
                            src = PT[:, half * 1024:(half + 1) * 1024].rearrange("p (j d) -> p j d", d=128)
                            dst = a1T.rearrange("p (ch c) -> p ch c", c=512)[:, half * 8:(half + 1) * 8,
                                                                            tt * 128:(tt + 1) * 128]
                            pt_use[half] = s["vC"].inc(v.tensor_copy(dst, src))

        if True:
            @blk.scalar
            def _(sc, vC0s=tuple(vC0s_l1), sC_l1=sC_l1):
                for tt in range(4):
                    zt = z_tg[:, tt * H:(tt + 1) * H]
                    _ln_scalar(nc, s, sc, zt, vC0s[tt], sC_l1 + 2 * tt, stats, consts)

        if tg == 0 and "d_a1" in dbg_h:
            if True:
                @blk.scalar
                def _(sc, v1=s["vC"].n):
                    sc.wait_ge(s["vC"].h, v1)
                    s["dbg"].inc(sc.dma_start(out=dbg_h["d_a1"][:, :], in_=z_tg[:, :]), 16)
                    s["dbg"].inc(sc.dma_start(out=dbg_h["d_a1T"][:, :], in_=a1T[:, :]), 16)

        cstk.close()

        if True:
            a1T_ready = s["vC"].n
            peC_f = s["peC"].n
            sC_f = s["sC"].n
            vC_f = s["vC"].n
            if True:
                @blk.gpsimd
                def _(g, tg=tg, peC_f=peC_f):
                    for it in range(2, NI):
                        gu_use[it % 2] = peC_f + (it - 1) * 2
                        load_gu(g, tg, it)
                    if tg == 0:
                        for it2 in range(2):
                            gu_use[it2 % 2] = peC_f + (NI - 1 + it2) * 2
                            load_gu(g, 1, it2)

                @blk.tensor
                def _(t, tg=tg, a1T_ready=a1T_ready, vC_f=vC_f, sC_f=sC_f):
                    for it in range(NI):
                        fs, fv = gu_fill[(tg, it)]
                        t.wait_ge(fs.h, fv)
                        if it == 0:
                            t.wait_ge(s["vC"].h, a1T_ready)
                        if it >= 2:
                            t.wait_ge(s["vC"].h, vC_f + it - 1)
                            t.wait_ge(s["sC"].h, sC_f + it - 1)
                        base = (it % 2) * 1024
                        for which, w_sb, off in (("g", gw_sb, 0), ("u", uw_sb, 512)):
                            for ch in range(NCH):
                                ins = t.matmul(PS4[:, base + off:base + off + 512],
                                               w_sb[it % 2][:, ch * 128:(ch + 1) * 128],
                                               a1T[:, ch * 512:(ch + 1) * 512],
                                               start=(ch == 0), stop=(ch == NCH - 1))
                                if ch == NCH - 1:
                                    s["peC"].inc(ins)

                @blk.scalar
                def _(sc, tg=tg, peC_f=peC_f, vC_f=vC_f):
                    if tg == 0:
                        sc.wait_ge(s["stgA"].h, 256)
                        sc.wait_ge(s["stgB"].h, 256)
                    for it in range(NI):
                        sc.wait_ge(s["vC"].h, vC_f + it)
                        sc.wait_ge(s["peC"].h, peC_f + 2 * it + 1)
                        ins = sc.activation(out=sg_bf[:, :],
                                            in_=PS4[:, (it % 2) * 1024:(it % 2) * 1024 + 512],
                                            func=AF.Silu, scale=gs_sb[:, it:it + 1])
                        s["sC"].inc(ins)

                @blk.vector
                def _(v, peC_f=peC_f, sC_f=sC_f):
                    for it in range(NI):
                        v.wait_ge(s["peC"].h, peC_f + 2 * it + 2)
                        v.wait_ge(s["sC"].h, sC_f + it + 1)
                        ins = v.scalar_tensor_tensor(
                            out=hdnT[:, it * 512:(it + 1) * 512],
                            in0=PS4[:, (it % 2) * 1024 + 512:(it % 2) * 1024 + 1024],
                            scalar=us_sb[:, it:it + 1], in1=sg_bf[:, :],
                            op0=ALU.mult, op1=ALU.mult)
                        s["vC"].inc(ins)

            hdn_all = s["vC"].n
            if tg == 0 and "d_hdnT" in dbg_h:
                if True:
                    @blk.scalar
                    def _(sc, hdn_all=hdn_all):
                        sc.wait_ge(s["vC"].h, hdn_all)
                        s["dbg"].inc(sc.dma_start(out=dbg_h["d_hdnT"][:, :], in_=hdnT[:, :]), 16)
            peC_d = s["peC"].n
            vC_d = s["vC"].n

            def _dcnt(nn):
                hh, jt = nn // 11, nn % 11
                return peC_d + hh * 14 + (jt + 1 if jt < 10 else 14)

            if True:
                @blk.gpsimd
                def _(g, tg=tg):
                    for nn in range(44):
                        if nn >= 2:
                            dw_use[nn % 2] = _dcnt(nn - 2)
                        load_dw(g, tg, (nn // 11), (nn % 11))
                    dw_use[0] = _dcnt(42)
                    dw_use[1] = _dcnt(43)

                @blk.tensor
                def _(t, tg=tg, hdn_all=hdn_all, vC_d=vC_d):
                    for hh in range(4):
                        for jt in range(11):
                            fs, fv = dw_fill[(tg, hh, jt)]
                            t.wait_ge(fs.h, fv)
                            if hh == 0 and jt == 0:
                                t.wait_ge(s["vC"].h, hdn_all)
                            if jt == 0 and hh > 0:
                                t.wait_ge(s["vC"].h, vC_d + (hh - 1) * 8 + 8)
                            for itl in range(4):
                                it = jt * 4 + itl
                                for tt in range(4):
                                    dst = PS4[:, tt * 512:(tt + 1) * 512]
                                    ins = t.matmul(dst,
                                                   hdnT[:, it * 512 + tt * 128:it * 512 + (tt + 1) * 128],
                                                   dw_sb[(hh * 11 + jt) % 2][:, itl * 512:(itl + 1) * 512],
                                                   start=(it == 0), stop=(it == NI - 1))
                                    if it == NI - 1:
                                        s["peC"].inc(ins)
                                    elif itl == 3 and tt == 3:
                                        s["peC"].inc(ins)

                @blk.vector
                def _(v, tg=tg):
                    if tg == 1:
                        v.wait_ge(s["out0"].h, 64)
                    for hh in range(4):
                        v.wait_ge(s["peC"].h, _dcnt(hh * 11 + 10))
                        for tt in range(4):
                            src_ = PS4[:, tt * 512:(tt + 1) * 512]
                            t2 = out_sb[0][:, hh * 512:(hh + 1) * 512]
                            v.wait_ge(s["vC"].h, s["vC"].n)
                            s["vC"].inc(v.tensor_tensor(t2, src_,
                                                        dsb_sb[:, hh * 512:(hh + 1) * 512], ALU.mult))
                            v.wait_ge(s["vC"].h, s["vC"].n)
                            ins = v.scalar_tensor_tensor(
                                out=z_tg[:, tt * H + hh * 512:tt * H + (hh + 1) * 512],
                                in0=t2, scalar=1.0,
                                in1=z_tg[:, tt * H + hh * 512:tt * H + (hh + 1) * 512],
                                op0=ALU.mult, op1=ALU.add)
                            s["vC"].inc(ins)

            if tg == 0 and "d_z2" in dbg_h:
                if True:
                    @blk.scalar
                    def _(sc, v1=s["vC"].n):
                        sc.wait_ge(s["vC"].h, v1)
                        s["dbg"].inc(sc.dma_start(out=dbg_h["d_z2"][:, :], in_=z_tg[:, :]), 16)

            sC_l2 = s["sC"].n
            vC0s_l2 = []
            apply_vals = []
            for tt in range(4):
                vC0 = s["vC"].n
                vC0s_l2.append(vC0)
                sC0 = sC_l2 + 2 * tt
                if True:
                    @blk.vector
                    def _(v, tg=tg, tt=tt, vC0=vC0, sC0=sC0):
                        zt = z_tg[:, tt * H:(tt + 1) * H]
                        prev_uses = 4 * tg + tt
                        if prev_uses > 0:
                            v.wait_ge(s["out0"].h, 16 * prev_uses)
                        _ln_vector(nc, s, v, zt, out_sb[tt % 2][:, :], vC0, sC0, stats, bnst)
                apply_vals.append(s["vC"].n)

            if True:
                @blk.scalar
                def _(sc, tg=tg, vC0s=tuple(vC0s_l2), sC_l2=sC_l2, applies=tuple(apply_vals)):
                    for tt in range(4):
                        zt = z_tg[:, tt * H:(tt + 1) * H]
                        _ln_scalar(nc, s, sc, zt, vC0s[tt], sC_l2 + 2 * tt, stats, consts)
                        sc.wait_ge(s["vC"].h, applies[tt])
                        sem = s["out0"]
                        sem.inc(
                            sc.dma_start(out=y[tg * 512 + tt * 128:tg * 512 + (tt + 1) * 128, :],
                                         in_=out_sb[tt % 2][:, :]), 16)
                        out_dma[(tg, tt)] = (sem, sem.n)

        hstk.close()

    if True:
        @blk.sync
        def _(sy):
            sy.wait_ge(s["out0"].h, s["out0"].n)
            sy.wait_ge(s["out1"].h, s["out1"].n)
            if dbg_h:
                sy.wait_ge(s["dbg"].h, s["dbg"].n)

        @blk.gpsimd
        def _(g):
            g.wait_ge(s["out0"].h, s["out0"].n)
            g.wait_ge(s["out1"].h, s["out1"].n)

    cst.close()
    st.close()
    return nc, dbg_h


def _ln_vector(nc, s, v, zt, out_ap, vC0, sC0, stats, bnst):
    v.wait_ge(s["vC"].h, vC0)
    ins = None
    for k in range(4):
        ins = v.bn_stats(bnst[:, k * 6:(k + 1) * 6], zt[:, k * 512:(k + 1) * 512])
    s["vC"].inc(ins)
    v.wait_ge(s["vC"].h, s["vC"].n)
    s["vC"].inc(v.bn_aggr(stats[:, 0:2], bnst[:, 0:24]))
    v.wait_ge(s["sC"].h, sC0 + 2)
    s["vC"].inc(v.tensor_scalar(out_ap, zt, stats[:, 0:1], stats[:, 3:4],
                                ALU.subtract, ALU.mult))


def _ln_scalar(nc, s, sc, zt, vC0, sC0, stats, consts):
    sc.wait_ge(s["vC"].h, vC0 + 2)
    s["sC"].inc(sc.activation(out=stats[:, 2:3], in_=stats[:, 1:2],
                              func=AF.Ln, bias=consts[:, 1:2]))
    sc.wait_ge(s["sC"].h, s["sC"].n)
    s["sC"].inc(sc.activation(out=stats[:, 3:4], in_=stats[:, 2:3],
                              func=AF.Exp, scale=-0.5))


def _host_inputs(inputs):
    x = np.ascontiguousarray(np.asarray(inputs["L0_attn_query"], dtype=np.float32))
    Wqkv = np.asarray(inputs["L0_attn_Wqkv"], dtype=np.float32)
    Wproj = np.asarray(inputs["L0_attn_Wproj"], dtype=np.float32)
    bproj = np.asarray(inputs["L0_attn_bproj"], dtype=np.float32)
    gate_w = np.asarray(inputs["L0_ffn_gate_w"], dtype=np.float32)
    up_w = np.asarray(inputs["L0_ffn_up_w"], dtype=np.float32)
    down_w = np.asarray(inputs["L0_ffn_down_w"], dtype=np.float32)
    gate_s = np.asarray(inputs["L0_ffn_gate_s"], dtype=np.float32)
    up_s = np.asarray(inputs["L0_ffn_up_s"], dtype=np.float32)
    down_s = np.asarray(inputs["L0_ffn_down_s"], dtype=np.float32)

    WqkvT = np.ascontiguousarray(Wqkv.T)
    xTb = [np.ascontiguousarray(x[b].T) for b in range(2)]
    wpT = np.ascontiguousarray(Wproj.reshape(8, 256, 16, 128).transpose(0, 3, 2, 1).reshape(8 * 128, 4096))
    gwT = np.ascontiguousarray(gate_w.reshape(44, 128, 16, 128).transpose(0, 3, 2, 1).reshape(44 * 128, 2048))
    uwT = np.ascontiguousarray(up_w.reshape(44, 128, 16, 128).transpose(0, 3, 2, 1).reshape(44 * 128, 2048))
    dwT = np.ascontiguousarray(down_w.reshape(4, 512, 11, 4, 128).transpose(0, 2, 4, 3, 1).reshape(44 * 128, 2048))
    gs2d = np.ascontiguousarray(gate_s.reshape(NI, 128).T)
    us2d = np.ascontiguousarray(up_s.reshape(NI, 128).T)
    dsb = np.ascontiguousarray(np.broadcast_to(down_s, (128, H)))
    bias2d = np.ascontiguousarray(bproj.reshape(1, H))
    ones_bf = np.ones((1, 128), dtype=ml_dtypes.bfloat16)
    ident_bf = np.eye(128, dtype=ml_dtypes.bfloat16)

    in_maps = []
    for c in range(8):
        beta = (c % 4) // 2
        par = 1 if c >= 4 else 0
        heads = [8 * (c % 2) + 2 * gi + par for gi in range(4)]
        q = c % 4

        def slab(base):
            cols = [WqkvT[:, h * 384 + base:h * 384 + base + 128] for h in heads]
            return np.ascontiguousarray(np.concatenate(cols, axis=1))

        idx = np.zeros((64, 1), dtype=np.uint32)
        for p in range(64):
            rank, gi, sub = p // 16, (p // 4) % 4, p % 4
            idx[p, 0] = rank * 128 + gi * 32 + q * 8 + sub * 2
        bp = c // 4
        in_maps.append({
            "xT": xTb[beta],
            "wq": slab(0), "wk": slab(128), "wv": slab(256),
            "wpT": wpT, "bias": bias2d,
            "gwT": gwT, "uwT": uwT, "dwT": dwT,
            "gs2d": gs2d, "us2d": us2d, "dsb_in": dsb,
            "xrows": np.ascontiguousarray(x[bp, q * 1024:(q + 1) * 1024, :]),
            "gidx": idx,
            "ones_in": ones_bf, "ident_in": ident_bf,
        })
    return in_maps


def kernel(**inputs):
    in_maps = _host_inputs(inputs)
    nc, _ = build_nc()
    res = run_bass_kernel_spmd(nc, in_maps, core_ids=list(range(8)))
    out = np.empty((2, 4096, 2048), dtype=np.float32)
    for c in range(8):
        bp, q = c // 4, c % 4
        out[bp, q * 1024:(q + 1) * 1024, :] = np.asarray(res.results[c]["y"])
    return out


if __name__ == "__main__":
    nc, _ = build_nc()
    print("built ok, instructions:", nc.n_instructions() if callable(getattr(nc, "n_instructions", None)) else "?")
